# version 15
# baseline (speedup 1.0000x reference)
import numpy as np
import concourse.bass as bass
import concourse.mybir as mybir

F32 = mybir.dt.float32
BF16 = mybir.dt.bfloat16
AF = mybir.ActivationFunctionType
ALU = mybir.AluOpType
AX = mybir.AxisListType

ENGINES = ("sync", "scalar", "vector", "tensor", "gpsimd")


def cdiv(a, b):
    return (a + b - 1) // b


class Buf:
    __slots__ = ("name", "w", "r")

    def __init__(self, name):
        self.name = name
        self.w = {}
        self.r = {}

    def reset(self):
        self.w = {}
        self.r = {}


def _acc(d, ev):
    if ev is not None:
        s, v = ev[0], ev[1]
        if d.get(s, 0) < v:
            d[s] = v


class Prog:
    def __init__(self, nc):
        self.nc = nc
        self.sems = {}
        self.cnt = {}
        self.ops = {e: [] for e in ENGINES}
        self.waited = {e: {} for e in ENGINES}
        self.bufs = {}
        self.nops = 0

    def sem(self, name):
        if name not in self.sems:
            self.sems[name] = self.nc.alloc_semaphore(name)
            self.cnt[name] = 0
        return self.sems[name]

    def buf(self, name):
        if name not in self.bufs:
            self.bufs[name] = Buf(name)
        return self.bufs[name]

    def do(self, eng, fn, reads=(), writes=(), extra=(), inc=None, n=1):
        assert inc is not None or (not reads and not writes)
        deps = {}
        for b in reads:
            for s, v in b.w.items():
                if deps.get(s, 0) < v:
                    deps[s] = v
        for b in writes:
            for d in (b.w, b.r):
                for s, v in d.items():
                    if deps.get(s, 0) < v:
                        deps[s] = v
        for d in extra:
            if d is not None and deps.get(d[0], 0) < d[1]:
                deps[d[0]] = d[1]
        ev = None
        if inc is not None:
            self.sem(inc)
            self.cnt[inc] += n
            ev = (inc, self.cnt[inc], n)
        self.ops[eng].append((fn, deps, ev))
        self.nops += 1
        for b in reads:
            _acc(b.r, ev)
        for b in writes:
            _acc(b.w, ev)
        return ev

    def flush_block(self):
        nc = self.nc
        with nc.Block() as blk:
            for ename in ENGINES:
                ops = self.ops[ename]
                if not ops:
                    continue

                def emitter(eng, _ops=ops, _ename=ename):
                    for fn, deps, ev in _ops:
                        for sname in sorted(deps):
                            val = deps[sname]
                            if _ename == "tensor" and sname == "t":
                                continue
                            if self.waited[_ename].get(sname, 0) >= val:
                                continue
                            eng.wait_ge(self.sems[sname], val)
                            self.waited[_ename][sname] = val
                        instr = fn(eng)
                        if ev is not None:
                            instr.then_inc(self.sems[ev[0]], ev[2])
                getattr(blk, ename)(emitter)
        self.ops = {e: [] for e in ENGINES}


def build(cfg):
    S = cfg["S"]
    D = cfg["D"]
    R = cfg["R"]
    K = cfg["K"]
    phases = cfg.get("phases", ("p1", "p2", "p3", "p4", "p5", "p6"))
    cut = cfg.get("cut", 99)
    last_phase = phases[-1]

    NT = S // 128
    NC = D // 128
    NSEG = cdiv(D, 512)
    G1 = 2
    TG = S // G1
    TGT = TG // 128
    NPAIR = K // 2
    assert 2 * R == 128 and TG <= 1024
    QG = min(512, S)
    NQG = S // QG

    eps = 1e-5
    smax_scale = float(np.float32(1.0) / np.sqrt(np.float32(D)))

    nc = bass.Bass("TRN2", target_bir_lowering=False, debug=False)
    p = Prog(nc)

    x_h = nc.dram_tensor("x", [S, D], F32, kind="ExternalInput").ap()
    qkv_h = nc.dram_tensor("qkv", [D, 3 * D], F32, kind="ExternalInput").ap()
    oproj_h = nc.dram_tensor("o_proj", [D, D], F32, kind="ExternalInput").ap()
    gamma_h = nc.dram_tensor("gamma", [D], F32, kind="ExternalInput").ap()
    beta_h = nc.dram_tensor("beta", [D], F32, kind="ExternalInput").ap()
    U_h = nc.dram_tensor("U", [K, D, R], F32, kind="ExternalInput").ap()
    V_h = nc.dram_tensor("V", [K, R, D], F32, kind="ExternalInput").ap()
    W1_h = nc.dram_tensor("W1", [D, D], F32, kind="ExternalInput").ap()
    b1_h = nc.dram_tensor("b1", [D], F32, kind="ExternalInput").ap()
    W2_h = nc.dram_tensor("W2", [D, K], F32, kind="ExternalInput").ap()
    b2_h = nc.dram_tensor("b2", [K], F32, kind="ExternalInput").ap()
    out_h = nc.dram_tensor("out", [S, D], F32, kind="ExternalOutput").ap()

    x1_h = nc.dram_tensor("x1_scratch", [S, D], F32).ap()
    xp_h = nc.dram_tensor("xp_scratch", [S, D], F32).ap()
    x2_h = nc.dram_tensor("x2_scratch", [S, D], F32).ap()
    q_hi_h = nc.dram_tensor("q_hi", [D, S], BF16).ap()
    q_lo_h = nc.dram_tensor("q_lo", [D, S], BF16).ap()
    k_hi_h = nc.dram_tensor("k_hi", [D, S], BF16).ap()
    k_lo_h = nc.dram_tensor("k_lo", [D, S], BF16).ap()
    v_hi_h = nc.dram_tensor("v_hi", [S, D], BF16).ap()
    v_lo_h = nc.dram_tensor("v_lo", [S, D], BF16).ap()
    pt_hi_h = nc.dram_tensor("pt_hi", [S, S], BF16).ap()
    pt_lo_h = nc.dram_tensor("pt_lo", [S, S], BF16).ap()
    aot_f_h = nc.dram_tensor("aot_f", [D, S], F32).ap()
    alpha_hb = nc.dram_tensor("alpha_scratch", [K, 1024], F32)
    alpha_h = alpha_hb.ap()
    lmax_hb = nc.dram_tensor("lmax_scratch", [1, 1024], F32)
    lmax_h = lmax_hb.ap()

    BIGW = max(NC * S, NC * (S // G1) + (K // 2) * NC * 128 + (K // 2) * D)
    bigA_hi = nc.alloc_sbuf_tensor("bigA_hi", [128, BIGW], BF16)
    bigA_lo = nc.alloc_sbuf_tensor("bigA_lo", [128, BIGW], BF16)
    xa0 = nc.alloc_sbuf_tensor("xa0", [128, D], F32)
    xa1 = nc.alloc_sbuf_tensor("xa1", [128, D], F32)
    ytile = nc.alloc_sbuf_tensor("ytile", [128, D], F32)
    ttile = nc.alloc_sbuf_tensor("ttile", [128, D], F32)
    ht_hi = nc.alloc_sbuf_tensor("ht_hi", [128, D], BF16)
    ht_lo = nc.alloc_sbuf_tensor("ht_lo", [128, D], BF16)
    ht2_hi = nc.alloc_sbuf_tensor("ht2_hi", [128, D], BF16)
    ht2_lo = nc.alloc_sbuf_tensor("ht2_lo", [128, D], BF16)
    sv_hi = nc.alloc_sbuf_tensor("sv_hi", [128, 2 * S], BF16)
    sv_lo = nc.alloc_sbuf_tensor("sv_lo", [128, 2 * S], BF16)
    albc = nc.alloc_sbuf_tensor("albc", [128, S], F32)

    w1f = nc.alloc_sbuf_tensor("w1f", [128, 512], F32)
    w1hi = nc.alloc_sbuf_tensor("w1hi", [128, 512], BF16)
    w1lo = nc.alloc_sbuf_tensor("w1lo", [128, 512], BF16)
    w2hi = nc.alloc_sbuf_tensor("w2hi", [128, NC * K], BF16)
    w2lo = nc.alloc_sbuf_tensor("w2lo", [128, NC * K], BF16)
    b1sb = nc.alloc_sbuf_tensor("b1sb", [128, NC], F32)
    gsb = nc.alloc_sbuf_tensor("gsb", [128, NC], F32)
    bsb = nc.alloc_sbuf_tensor("bsb", [128, NC], F32)
    b2sb = nc.alloc_sbuf_tensor("b2sb", [128, 1], F32)
    ident = nc.alloc_sbuf_tensor("ident", [128, 128], F32)
    identb = nc.alloc_sbuf_tensor("identb", [128, 128], BF16)
    maskt = nc.alloc_sbuf_tensor("maskt", [128, 128], F32)
    stats = nc.alloc_sbuf_tensor("stats", [128, 6 * NSEG], F32)
    stats2 = nc.alloc_sbuf_tensor("stats2", [128, 6 * NSEG], F32)
    mvt = nc.alloc_sbuf_tensor("mvt", [128, 2], F32)
    sdt = nc.alloc_sbuf_tensor("sdt", [128, 1], F32)
    rstd = nc.alloc_sbuf_tensor("rstd", [128, 1], F32)
    mvt2 = nc.alloc_sbuf_tensor("mvt2", [128, 2], F32)
    sdt2 = nc.alloc_sbuf_tensor("sdt2", [128, 1], F32)
    rstd2 = nc.alloc_sbuf_tensor("rstd2", [128, 1], F32)
    nmax = nc.alloc_sbuf_tensor("nmax", [128, 1], F32)
    lsum = nc.alloc_sbuf_tensor("lsum", [128, 1], F32)
    rinv = nc.alloc_sbuf_tensor("rinv", [128, 1], F32)
    RW = min(1024, S // 2)
    assert TG <= RW
    logt = albc[0:K, RW:2 * RW]
    alphaT = logt
    lmax = albc[0:1, 0:RW]
    lmaxB = albc[0:K, 0:RW]
    ztmp = nc.alloc_sbuf_tensor("ztmp", [128, 128], F32)
    epssb = nc.alloc_sbuf_tensor("epssb", [128, 1], F32)
    ptb = nc.alloc_sbuf_tensor("ptb", [128, 256], BF16)
    zbf = nc.alloc_sbuf_tensor("zbf", [128, 128], BF16)
    ones8m = nc.alloc_sbuf_tensor("ones8m", [K, 1], F32)
    ones8s = nc.alloc_sbuf_tensor("ones8s", [K, 1], F32)

    psA = nc.alloc_psum_tensor("psA", [128, 2048], F32)
    psB = nc.alloc_psum_tensor("psB", [128, 1024], F32)
    psC = nc.alloc_psum_tensor("psC", [128, 512], F32)
    psT = nc.alloc_psum_tensor("psT", [128, 1024], BF16)
    PB = [p.buf("PB%d" % i) for i in range(8)]

    def pbanks(base, c0, c1):
        return PB[base + c0 // 512: base + cdiv(c1, 512)]

    def psa(c0, c1):
        return psA[:, c0:c1], pbanks(0, c0, c1)

    def psb(c0, c1):
        return psB[:, c0:c1], pbanks(4, c0, c1)

    def psc(c0, c1):
        return psC[:, c0:c1], pbanks(6, c0, c1)

    gb = p.buf

    def dma_in(dst_ap, src_ap, writes, reads=(), slow=False):
        sem = "ldB_" + writes[0].name
        if slow:
            return p.do("sync", lambda e, d=dst_ap, s=src_ap: e.dma_start(out=d, in_=s, allow_slow_non_contiguous=True),
                        reads=reads, writes=writes, inc=sem, n=16)
        return p.do("sync", lambda e, d=dst_ap, s=src_ap: e.dma_start(out=d, in_=s),
                    reads=reads, writes=writes, inc=sem, n=16)

    def dma_out(dst_ap, src_ap, reads, writes=()):
        sem = "stB_" + reads[0].name
        return p.do("gpsimd", lambda e, d=dst_ap, s=src_ap: e.dma_start(out=d, in_=s),
                    reads=reads, writes=writes, inc=sem, n=16)

    def dma_acc(dst_ap, src_ap, reads, writes=()):
        sem = "stB_" + reads[0].name
        return p.do("gpsimd", lambda e, d=dst_ap, s=src_ap: e.dma_start(out=d, in_=s, accum_op=ALU.add),
                    reads=reads, writes=writes, inc=sem, n=16)

    def dma_out_hw(dst_ap, src_ap, reads, writes=()):
        sem = "shB_" + reads[0].name
        return p.do("scalar", lambda e, d=dst_ap, s=src_ap: e.dma_start(out=d, in_=s),
                    reads=reads, writes=writes, inc=sem, n=16)

    def dma_in_hw(dst_ap, src_ap, writes, reads=()):
        sem = "lhB_" + writes[0].name
        return p.do("scalar", lambda e, d=dst_ap, s=src_ap: e.dma_start(out=d, in_=s),
                    reads=reads, writes=writes, inc=sem, n=16)

    def ln_stats(src, src_buf, mv, mvb, sd, sdb, rs, rsb, stt=None, sttb=None):
        if stt is None:
            stt, sttb = stats, "stats"
        segb = []
        for sgi in range(NSEG):
            a, b = sgi * 512, min(D, (sgi + 1) * 512)
            sb_ = gb("%s_%d" % (sttb if isinstance(sttb, str) else sttb.name, sgi))
            segb.append(sb_)
            p.do("vector", lambda e, s=src, a=a, b=b, i=sgi, stt=stt: e.bn_stats(
                out=stt[:, 6 * i:6 * (i + 1)], in_=s[:, a:b]),
                reads=[src_buf], writes=[sb_], inc="v")
        p.do("vector", lambda e, mv=mv, stt=stt: e.bn_aggr(out=mv[:], in_=stt[:, 0:6 * NSEG]),
             reads=segb, writes=[mvb], inc="v")
        p.do("scalar", lambda e, mv=mv, sd=sd: e.activation(out=sd[:], in_=mv[:, 1:2], func=AF.Sqrt, bias=epssb[:], scale=1.0),
             reads=[mvb, gb("epssb")], writes=[sdb], inc="s")
        p.do("vector", lambda e, sd=sd, rs=rs: e.reciprocal(out=rs[:], in_=sd[:]),
             reads=[sdb], writes=[rsb], inc="v")

    def split_gp(src_ap, src_bufs, hi_ap, hi_buf, lo_ap, lo_buf):
        p.do("gpsimd", lambda e: e.tensor_copy(out=hi_ap, in_=src_ap),
             reads=src_bufs, writes=[hi_buf], inc="g")
        p.do("gpsimd", lambda e: e.tensor_tensor(out=lo_ap, in0=src_ap, in1=hi_ap, op=ALU.subtract),
             reads=list(src_bufs) + [hi_buf], writes=[lo_buf], inc="g")

    def split_hilo(src_ap, src_bufs, hi_ap, hi_buf, lo_ap, lo_buf, func=AF.Identity, bias=None, scale=1.0):
        if bias is None:
            p.do("scalar", lambda e: e.activation(out=hi_ap, in_=src_ap, func=func, scale=scale),
                 reads=src_bufs, writes=[hi_buf], inc="s")
        else:
            p.do("scalar", lambda e: e.activation(out=hi_ap, in_=src_ap, func=func, bias=bias, scale=scale),
                 reads=src_bufs, writes=[hi_buf], inc="s")
        p.do("vector", lambda e: e.tensor_tensor(out=lo_ap, in0=src_ap, in1=hi_ap, op=ALU.subtract),
             reads=list(src_bufs) + [hi_buf], writes=[lo_buf], inc="v")

    def prep_consts():
        p.do("gpsimd", lambda e: e.memset(ztmp[:], 1.0), writes=[gb("ztmp")], inc="g")
        p.do("gpsimd", lambda e: e.memset(ones8m[:], 1.0 / K), writes=[gb("ones8m")], inc="g")
        p.do("gpsimd", lambda e: e.memset(ones8s[:], 1.0), writes=[gb("ones8s")], inc="g")
        p.do("gpsimd", lambda e: e.memset(epssb[:], eps), writes=[gb("epssb")], inc="g")
        p.do("gpsimd", lambda e: e.memset(maskt[:], 0.0), writes=[gb("maskt")], inc="g")
        p.do("gpsimd", lambda e: e.affine_select(out=ident[:], in_=ztmp[:], pattern=[[-1, 128]],
                                                 compare_op=ALU.is_equal, fill=0.0, base=0, channel_multiplier=1),
             reads=[gb("ztmp")], writes=[gb("ident")], inc="g")
        p.do("gpsimd", lambda e: e.affine_select(out=maskt[:], in_=maskt[:], pattern=[[-1, 128]],
                                                 compare_op=ALU.is_ge, fill=-1e30, base=0, channel_multiplier=1),
             writes=[gb("maskt")], inc="g")
        p.do("vector", lambda e: e.tensor_copy(out=identb[:], in_=ident[:]),
             reads=[gb("ident")], writes=[gb("identb")], inc="v")
        for name, hsrc, dst in (("b1sb", b1_h, b1sb), ("gsb", gamma_h, gsb), ("bsb", beta_h, bsb)):
            ap_, bk = psc(0, 512)
            dma_in(ytile[0:NC, 0:128], hsrc.rearrange("(a b) -> a b", b=128), writes=[gb("ytile")])
            p.do("tensor", lambda e: e.transpose(out=psC[:, 0:NC], in_=ytile[0:NC, 0:128], identity=ident[0:NC, 0:NC]),
                 reads=[gb("ytile"), gb("ident")], writes=bk, inc="t")
            p.do("vector", lambda e, d=dst: e.tensor_copy(out=d[:, 0:NC], in_=psC[:, 0:NC]),
                 reads=bk, writes=[gb(name)], inc="v")
        dma_in(ytile[0:1, 0:K], b2_h.rearrange("(a b) -> a b", a=1), writes=[gb("ytile")])
        p.do("tensor", lambda e: e.transpose(out=psC[0:K, 0:1], in_=ytile[0:1, 0:K], identity=ident[0:1, 0:1]),
             reads=[gb("ytile"), gb("ident")], writes=pbanks(6, 0, 512), inc="t")
        p.do("vector", lambda e: e.tensor_copy(out=b2sb[0:K, :], in_=psC[0:K, 0:1]),
             reads=pbanks(6, 0, 512), writes=[gb("b2sb")], inc="v")
        for c in range(NC):
            dma_in(ttile[0:128, c * K:(c + 1) * K], W2_h[c * 128:(c + 1) * 128, :], writes=[gb("ttile")], slow=True)
        split_hilo(ttile[:, 0:NC * K], [gb("ttile")], w2hi[:, 0:NC * K], gb("w2hi"), w2lo[:, 0:NC * K], gb("w2lo"))

    VT_BASE = NC * TG
    UT_BASE = VT_BASE + NPAIR * NC * 128

    def vt_slot(c, dc):
        return VT_BASE + (c * NC + dc) * 128

    def ut_base(c):
        return UT_BASE + c * D

    def prep_vu():
        for k in range(K):
            c, odd = divmod(k, 2)
            vb = (ytile, ttile)[k % 2]
            vbb = gb(("ytile", "ttile")[k % 2])
            dma_in(vb[0:R, :], V_h[k, :, :], writes=[vbb])

            def tr_v(e, vb=vb):
                last = None
                for dc in range(NC):
                    last = e.transpose(out=psA[:, dc * R:(dc + 1) * R],
                                       in_=vb[0:R, dc * 128:(dc + 1) * 128], identity=ident[0:R, 0:R])
                return last
            pabk = pbanks(0, 0, NC * R)
            p.do("tensor", tr_v, reads=[vbb, gb("ident")], writes=pabk, inc="t")
            hi_out = bass.AP(bigA_hi, VT_BASE + c * NC * 128 + odd * R, [[BIGW, 128], [128, NC], [1, R]])
            lo_out = bass.AP(bigA_lo, VT_BASE + c * NC * 128 + odd * R, [[BIGW, 128], [128, NC], [1, R]])
            p.do("scalar", lambda e, hi_out=hi_out: e.activation(out=hi_out, in_=psA[:, 0:NC * R], func=AF.Identity),
                 reads=pabk, writes=[gb("bigA_hi")], inc="s")
            p.do("vector", lambda e, hi_out=hi_out, lo_out=lo_out: e.tensor_tensor(
                out=lo_out, in0=psA[:, 0:NC * R], in1=hi_out, op=ALU.subtract),
                reads=pabk + [gb("bigA_hi")], writes=[gb("bigA_lo")], inc="v")
        for k in range(K):
            c, odd = divmod(k, 2)
            ub = (xa0, xa1)[k % 2]
            ubb = gb("xa%d" % (k % 2))
            uin = bass.AP(U_h.tensor, k * D * R, [[R, 128], [128 * R, NC], [1, R]])
            (dma_in if k % 2 == 0 else dma_in_hw)(ub[:, 0:NC * R], uin, writes=[ubb])

            def tr_u(e, ub=ub):
                last = None
                for dc in range(NC):
                    last = e.transpose(out=psA[0:R, dc * 128:(dc + 1) * 128],
                                       in_=ub[:, dc * R:(dc + 1) * R], identity=ident[:])
                return last
            pabk = pbanks(0, 0, NC * 128)
            p.do("tensor", tr_u, reads=[ubb, gb("ident")], writes=pabk, inc="t")
            p.do("scalar", lambda e, odd=odd, c=c: e.activation(
                out=bigA_hi[odd * R:odd * R + R, ut_base(c):ut_base(c) + D], in_=psA[0:R, 0:D], func=AF.Identity),
                reads=pabk, writes=[gb("bigA_hi")], inc="s")
            p.do("vector", lambda e, odd=odd, c=c: e.tensor_tensor(
                out=bigA_lo[odd * R:odd * R + R, ut_base(c):ut_base(c) + D], in0=psA[0:R, 0:D],
                in1=bigA_hi[odd * R:odd * R + R, ut_base(c):ut_base(c) + D], op=ALU.subtract),
                reads=pabk + [gb("bigA_hi")], writes=[gb("bigA_lo")], inc="v")

    def ffn_phase(src_h, dst_h, srcb, dstb, dtag):
        yTh, yTl = gb("bigA_hi"), gb("bigA_lo")
        for g in range(G1):
            t0 = g * TGT
            for i in range(TGT):
                ti = t0 + i
                par = i % 2
                xa = (xa0, xa1)[par]
                xab = gb("xa%d" % par)
                yb = (ytile, ttile)[par]
                ybb = gb(("ytile", "ttile")[par])
                mv_, mvb_ = (mvt, mvt2)[par], (gb("mvt"), gb("mvt2"))[par]
                sd_, sdb_ = (sdt, sdt2)[par], (gb("sdt"), gb("sdt2"))[par]
                rs_, rsb_ = (rstd, rstd2)[par], (gb("rstd"), gb("rstd2"))[par]
                stt_ = (stats, stats2)[par]
                dma_in(xa[:], src_h[ti * 128:(ti + 1) * 128, :], writes=[xab], reads=[srcb])
                ln_stats(xa, xab, mv_, mvb_, sd_, sdb_, rs_, rsb_, stt_, ("stats", "stats2")[par])
                p.do("vector", lambda e, xa=xa, yb=yb, mv_=mv_, rs_=rs_: e.tensor_scalar(
                    out=yb[:], in0=xa[:], scalar1=mv_[:, 0:1], scalar2=rs_[:],
                    op0=ALU.subtract, op1=ALU.mult),
                    reads=[xab, mvb_, rsb_], writes=[ybb], inc="v")

                def tr_all(e, yb=yb):
                    last = None
                    for dc in range(NC):
                        last = e.transpose(out=psA[:, dc * 128:(dc + 1) * 128],
                                           in_=yb[:, dc * 128:(dc + 1) * 128], identity=ident[:])
                    return last
                pabk = pbanks(0, 0, NC * 128)
                p.do("tensor", tr_all, reads=[ybb, gb("ident")], writes=pabk, inc="t")
                hi_out = bass.AP(bigA_hi, ti % TGT * 0 + i * 128, [[BIGW, 128], [TG, NC], [1, 128]])
                lo_out = bass.AP(bigA_lo, i * 128, [[BIGW, 128], [TG, NC], [1, 128]])
                p.do("scalar", lambda e, hi_out=hi_out: e.activation(
                    out=hi_out, in_=psA[:, 0:NC * 128], func=AF.Identity),
                    reads=pabk, writes=[yTh], inc="s")
                p.do("vector", lambda e, hi_out=hi_out, lo_out=lo_out: e.tensor_tensor(
                    out=lo_out, in0=psA[:, 0:NC * 128], in1=hi_out, op=ALU.subtract),
                    reads=pabk + [yTh], writes=[yTl], inc="v")

            if cut < 3:
                continue
            hb = [psa(0, TG), psa(1024, 1024 + TG)]
            hsb = [(ht_hi, ht_lo, gb("ht_hi"), gb("ht_lo")), (ht2_hi, ht2_lo, gb("ht2_hi"), gb("ht2_lo"))]
            lg_ap, lg_bk = psb(0, TG)
            pend_router = []

            def emit_router():
                if pend_router:
                    fn, rd = pend_router.pop()
                    p.do("tensor", fn, reads=rd, writes=lg_bk, inc="t")

            chunks = [(do_, di) for do_ in range(NC) for di in range(NC)]
            if cfg.get("maxmm") is not None:
                chunks = chunks[:cfg.get("maxmm")]

            def w1_load(j):
                do_, di = chunks[j]
                ws = j % 4
                dma_in(w1f[:, ws * 128:(ws + 1) * 128],
                       W1_h[di * 128:(di + 1) * 128, do_ * 128:(do_ + 1) * 128], writes=[gb("w1f%d" % ws)])
                split_gp(w1f[:, ws * 128:(ws + 1) * 128], [gb("w1f%d" % ws)],
                         w1hi[:, ws * 128:(ws + 1) * 128], gb("w1hi%d" % ws),
                         w1lo[:, ws * 128:(ws + 1) * 128], gb("w1lo%d" % ws))

            for j in range(min(4, len(chunks))):
                w1_load(j)
            for j, (do_, di) in enumerate(chunks):
                ws = j % 4
                pa, pab = hb[do_ % 2]

                def mm_h(e, di=di, ws=ws, pa=pa):
                    last = None
                    for pi, (wop, yop) in enumerate(((w1hi, bigA_hi), (w1hi, bigA_lo), (w1lo, bigA_hi))):
                        for nt in range(cdiv(TG, 512)):
                            n0, n1 = nt * 512, min(TG, (nt + 1) * 512)
                            last = e.matmul(out=pa[:, n0:n1], lhsT=wop[:, ws * 128:(ws + 1) * 128],
                                            rhs=yop[:, di * TG + n0:di * TG + n1],
                                            start=(di == 0 and pi == 0), stop=(di == NC - 1 and pi == 2))
                    return last
                p.do("tensor", mm_h, reads=[gb("w1hi%d" % ws), gb("w1lo%d" % ws), yTh, yTl], writes=pab, inc="t")
                if j + 4 < len(chunks):
                    w1_load(j + 4)
                if di == NC - 1:
                    hh, hl, hhb, hlb = hsb[do_ % 2]
                    p.do("vector", lambda e, pa=pa, do_=do_: e.tensor_scalar(
                        out=ttile[:, 0:TG], in0=pa[:, 0:TG], scalar1=b1sb[:, do_:do_ + 1], scalar2=0.0,
                        op0=ALU.add, op1=ALU.max),
                        reads=pab + [gb("b1sb")], writes=[gb("ttile")], inc="v")
                    p.do("scalar", lambda e, hh=hh: e.activation(
                        out=hh[:, 0:TG], in_=ttile[:, 0:TG], func=AF.Identity),
                        reads=[gb("ttile")], writes=[hhb], inc="s")
                    p.do("vector", lambda e, hh=hh, hl=hl: e.tensor_tensor(
                        out=hl[:, 0:TG], in0=ttile[:, 0:TG], in1=hh[:, 0:TG], op=ALU.subtract),
                        reads=[gb("ttile"), hhb], writes=[hlb], inc="v")

                    def mm_router(e, do_=do_, hh=hh, hl=hl):
                        last = None
                        for pi, (wop, hop) in enumerate(((w2hi, hh), (w2hi, hl), (w2lo, hh))):
                            for nt in range(cdiv(TG, 512)):
                                n0, n1 = nt * 512, min(TG, (nt + 1) * 512)
                                last = e.matmul(out=psB[0:K, n0:n1], lhsT=wop[:, do_ * K:(do_ + 1) * K],
                                                rhs=hop[:, n0:n1],
                                                start=(do_ == 0 and pi == 0), stop=(do_ == NC - 1 and pi == 2))
                        return last
                    emit_router()
                    if not cfg.get("no_router"):
                        pend_router.append((mm_router, [hhb, hlb, gb("w2hi"), gb("w2lo")]))
            emit_router()

            if cut < 4:
                continue
            p.do("scalar", lambda e: e.activation(out=logt[0:K, 0:TG], in_=psB[0:K, 0:TG], func=AF.Identity,
                                                  bias=b2sb[0:K, :], scale=1.0),
                 reads=lg_bk + [gb("b2sb")], writes=[gb("albc")], inc="s")
            def bcast_via_hbm(src_sb, src_buf, hbt, hb_ap, nreps, cols):
                dma_in(hb_ap[0:1, 0:cols], src_sb, writes=[gb(hbt)], reads=[src_buf])
                return bass.AP(hb_ap.tensor, hb_ap.offset, [[0, nreps], [1, cols]])

            def mm_mean(e):
                last = None
                for nt in range(cdiv(TG, 512)):
                    n0, n1 = nt * 512, min(TG, (nt + 1) * 512)
                    last = e.matmul(out=psB[0:1, n0:n1], lhsT=ones8m[0:K, :], rhs=logt[0:K, n0:n1],
                                    start=True, stop=True)
                return last
            p.do("tensor", mm_mean, reads=[gb("albc"), gb("ones8m")], writes=lg_bk, inc="t")
            p.do("vector", lambda e: e.tensor_copy(out=lmax[0:1, 0:TG], in_=psB[0:1, 0:TG]),
                 reads=lg_bk, writes=[gb("albc")], inc="v")
            rep = bcast_via_hbm(lmax[0:1, 0:TG], gb("albc"), "lmax_h", lmax_h, K, TG)
            dma_in(lmaxB[0:K, 0:TG], rep, writes=[gb("albc")], reads=[gb("lmax_h")])
            p.do("vector", lambda e: e.tensor_tensor(out=logt[0:K, 0:TG], in0=logt[0:K, 0:TG],
                                                     in1=lmaxB[0:K, 0:TG], op=ALU.subtract),
                 reads=[gb("albc")], writes=[gb("albc")], inc="v")
            p.do("scalar", lambda e: e.activation(out=logt[0:K, 0:TG], in_=logt[0:K, 0:TG], func=AF.Exp),
                 writes=[gb("albc")], inc="s")
            def mm_sum(e):
                last = None
                for nt in range(cdiv(TG, 512)):
                    n0, n1 = nt * 512, min(TG, (nt + 1) * 512)
                    last = e.matmul(out=psB[0:1, n0:n1], lhsT=ones8s[0:K, :], rhs=logt[0:K, n0:n1],
                                    start=True, stop=True)
                return last
            p.do("tensor", mm_sum, reads=[gb("albc"), gb("ones8s")], writes=lg_bk, inc="t")
            p.do("vector", lambda e: e.reciprocal(out=lmax[0:1, 0:TG], in_=psB[0:1, 0:TG]),
                 reads=lg_bk, writes=[gb("albc")], inc="v")
            rep = bcast_via_hbm(lmax[0:1, 0:TG], gb("albc"), "lmax_h", lmax_h, K, TG)
            dma_in(lmaxB[0:K, 0:TG], rep, writes=[gb("albc")], reads=[gb("lmax_h")])
            p.do("vector", lambda e: e.tensor_tensor(out=alphaT[0:K, 0:TG], in0=logt[0:K, 0:TG],
                                                     in1=lmaxB[0:K, 0:TG], op=ALU.mult),
                 reads=[gb("albc"), gb("albc")], writes=[gb("albc")], inc="v")
            dma_in(alpha_h[0:K, 0:TG], alphaT[0:K, 0:TG], writes=[gb("alpha_h")], reads=[gb("albc")])

            if cut < 5:
                continue
            for c in range(NPAIR):
                pa, pab = hb[c % 2]
                rep0 = bass.AP(alpha_hb, (2 * c) * 1024, [[0, R], [1, TG]])
                rep1 = bass.AP(alpha_hb, (2 * c + 1) * 1024, [[0, 128 - R], [1, TG]])
                dma_in(albc[0:R, 0:TG], rep0, writes=[gb("albc")], reads=[gb("alpha_h")])
                dma_in(albc[R:128, 0:TG], rep1, writes=[gb("albc")], reads=[gb("alpha_h")])

                def mm_vx(e, c=c, pa=pa):
                    last = None
                    for pi, (wsel, ysel) in enumerate(((bigA_hi, bigA_hi), (bigA_hi, bigA_lo), (bigA_lo, bigA_hi))):
                        for di in range(NC):
                            for nt in range(cdiv(TG, 512)):
                                n0, n1 = nt * 512, min(TG, (nt + 1) * 512)
                                last = e.matmul(out=pa[:, n0:n1],
                                                lhsT=wsel[:, vt_slot(c, di):vt_slot(c, di) + 128],
                                                rhs=ysel[:, di * TG + n0:di * TG + n1],
                                                start=(pi == 0 and di == 0), stop=(pi == 2 and di == NC - 1))
                    return last
                p.do("tensor", mm_vx, reads=[yTh, yTl], writes=pab, inc="t")
                p.do("vector", lambda e, pa=pa: e.tensor_tensor(out=ttile[:, 0:TG], in0=pa[:, 0:TG],
                                                                in1=albc[:, 0:TG], op=ALU.mult),
                     reads=pab + [gb("albc")], writes=[gb("ttile")], inc="v")
                split_hilo(ttile[:, 0:TG], [gb("ttile")], sv_hi[:, c * TG:(c + 1) * TG], gb("sv_hi"),
                           sv_lo[:, c * TG:(c + 1) * TG], gb("sv_lo"))

            if cut < 6:
                continue
            DH = min(D, 1024)
            NDH = cdiv(D, DH)
            rot = [psa(0, DH), psa(1024, 1024 + DH), psb(0, DH)]
            ri = 0
            for i in range(TGT):
                ti = t0 + i
                xr = (ytile, ttile)[i % 2]
                xrb = gb(("ytile", "ttile")[i % 2])
                dma_in(xr[:], src_h[ti * 128:(ti + 1) * 128, :], writes=[xrb], reads=[srcb])
                ob = (xa0, xa1)[i % 2]
                obb = gb("xa%d" % (i % 2))
                for dh in range(NDH):
                    pa, pab = rot[ri % 3]
                    ri += 1

                    DW = min(DH, D - dh * DH)

                    def mm_uvx(e, i=i, dh=dh, pa=pa, DW=DW):
                        last = None
                        for pi, (ssel, usel) in enumerate(((sv_hi, bigA_hi), (sv_hi, bigA_lo), (sv_lo, bigA_hi))):
                            for c in range(NPAIR):
                                for nt in range(cdiv(DW, 512)):
                                    n0, n1 = nt * 512, min(DW, (nt + 1) * 512)
                                    last = e.matmul(
                                        out=pa[:, n0:n1],
                                        lhsT=ssel[:, c * TG + i * 128:c * TG + (i + 1) * 128],
                                        rhs=usel[:, ut_base(c) + dh * DH + n0:ut_base(c) + dh * DH + n1],
                                        start=(pi == 0 and c == 0), stop=(pi == 2 and c == NPAIR - 1))
                        return last
                    p.do("tensor", mm_uvx, reads=[gb("sv_hi"), gb("sv_lo"), yTh, yTl], writes=pab, inc="t")
                    p.do("vector", lambda e, pa=pa, dh=dh, xr=xr, ob=ob, DW=DW: e.scalar_tensor_tensor(
                        out=ob[:, dh * DH:dh * DH + DW], in0=pa[:, 0:DW], scalar=0.5,
                        in1=xr[:, dh * DH:dh * DH + DW], op0=ALU.mult, op1=ALU.add),
                        reads=pab + [xrb], writes=[obb], inc="v")
                dma_out(dst_h[ti * 128:(ti + 1) * 128, :], ob[:], reads=[obb], writes=[dstb])
        p.flush_block()


    def proj_phase():
        zTh, zTl = gb("bigA_hi"), gb("bigA_lo")
        THW = min(1024, S)
        NTH = cdiv(S, THW)
        TPH = THW // 128

        def emit_a_ln(ti):
            xa = (xa0, xa1)[ti % 2]
            xab = gb("xa%d" % (ti % 2))
            dma_in(xa[:], x1_h[ti * 128:(ti + 1) * 128, :], writes=[xab], reads=[gb("x1_h")])
            ln_stats(xa, xab, mvt, gb("mvt"), sdt, gb("sdt"), rstd, gb("rstd"))
            p.do("vector", lambda e, xa=xa: e.tensor_scalar(
                out=ytile[:], in0=xa[:], scalar1=mvt[:, 0:1], scalar2=rstd[:],
                op0=ALU.subtract, op1=ALU.mult),
                reads=[xab, gb("mvt"), gb("rstd")], writes=[gb("ytile")], inc="v")
            ln_stats(ytile, gb("ytile"), mvt2, gb("mvt2"), sdt2, gb("sdt2"), rstd2, gb("rstd2"),
                     stats2, "stats2")
            p.do("vector", lambda e: e.tensor_scalar(
                out=ttile[:], in0=ytile[:], scalar1=mvt2[:, 0:1], scalar2=rstd2[:],
                op0=ALU.subtract, op1=ALU.mult),
                reads=[gb("ytile"), gb("mvt2"), gb("rstd2")], writes=[gb("ttile")], inc="v")
            p.do("vector", lambda e, xa=xa: e.tensor_tensor(out=xa[:], in0=xa[:], in1=ytile[:], op=ALU.add),
                 reads=[gb("ytile")], writes=[xab], inc="v")
            dma_out(xp_h[ti * 128:(ti + 1) * 128, :], xa[:], reads=[xab], writes=[gb("xp_h")])

        def emit_a_tr(ti, use_psc):
            if not use_psc:
                def tr_all(e):
                    last = None
                    for dc in range(NC):
                        last = e.transpose(out=psA[:, dc * 128:(dc + 1) * 128],
                                           in_=ttile[:, dc * 128:(dc + 1) * 128], identity=ident[:])
                    return last
                pabk = pbanks(0, 0, NC * 128)
                p.do("tensor", tr_all, reads=[gb("ttile"), gb("ident")], writes=pabk, inc="t")
                for dc in range(NC):
                    p.do("scalar", lambda e, dc=dc: e.activation(
                        out=albc[:, dc * 128:(dc + 1) * 128], in_=psA[:, dc * 128:(dc + 1) * 128],
                        func=AF.Identity, bias=bsb[:, dc:dc + 1], scale=gsb[:, dc:dc + 1]),
                        reads=pabk + [gb("gsb"), gb("bsb")], writes=[gb("albc")], inc="s")
            else:
                BCH = max(1, NC // 4)
                for b0 in range(0, NC, BCH):
                    nb = min(BCH, NC - b0)
                    def tr_b(e, b0=b0, nb=nb):
                        last = None
                        for j in range(nb):
                            last = e.transpose(out=psC[:, j * 128:(j + 1) * 128],
                                               in_=ttile[:, (b0 + j) * 128:(b0 + j + 1) * 128],
                                               identity=ident[:])
                        return last
                    p.do("tensor", tr_b, reads=[gb("ttile"), gb("ident")], writes=[PB[6]], inc="t")
                    for j in range(nb):
                        p.do("scalar", lambda e, b0=b0, j=j: e.activation(
                            out=albc[:, (b0 + j) * 128:(b0 + j + 1) * 128], in_=psC[:, j * 128:(j + 1) * 128],
                            func=AF.Identity, bias=bsb[:, b0 + j:b0 + j + 1], scale=gsb[:, b0 + j:b0 + j + 1]),
                            reads=[PB[6], gb("gsb"), gb("bsb")], writes=[gb("albc")], inc="s")
            hi_out = bass.AP(bigA_hi, ti * 128, [[BIGW, 128], [S, NC], [1, 128]])
            lo_out = bass.AP(bigA_lo, ti * 128, [[BIGW, 128], [S, NC], [1, 128]])
            p.do("scalar", lambda e, hi_out=hi_out: e.activation(
                out=hi_out, in_=albc[:, 0:NC * 128], func=AF.Identity),
                reads=[gb("albc")], writes=[zTh], inc="s")
            p.do("vector", lambda e, hi_out=hi_out, lo_out=lo_out: e.tensor_tensor(
                out=lo_out, in0=albc[:, 0:NC * 128], in1=hi_out, op=ALU.subtract),
                reads=[gb("albc"), zTh], writes=[zTl], inc="v")

        for ti in range(TPH):
            emit_a_ln(ti)
            emit_a_tr(ti, use_psc=False)

        pend = []

        def pop_pend(n=1):
            for _ in range(n):
                if pend:
                    pend.pop(0)()
        for ti in range(TPH, NT):
            pend.append(lambda ti=ti: emit_a_ln(ti))
            pend.append(lambda ti=ti: emit_a_tr(ti, use_psc=True))

        chunks = [(th, do_, di) for th in range(NTH) for do_ in range(3 * NC) for di in range(NC)]

        def qkv_load(j):
            th, do_, di = chunks[j]
            ws = j % 4
            dma_in(w1f[:, ws * 128:(ws + 1) * 128],
                   qkv_h[di * 128:(di + 1) * 128, do_ * 128:(do_ + 1) * 128], writes=[gb("w1f%d" % ws)])
            split_hilo(w1f[:, ws * 128:(ws + 1) * 128], [gb("w1f%d" % ws)],
                       w1hi[:, ws * 128:(ws + 1) * 128], gb("w1hi%d" % ws),
                       w1lo[:, ws * 128:(ws + 1) * 128], gb("w1lo%d" % ws))

        for j in range(min(4, len(chunks))):
            qkv_load(j)
        vpend = []
        rot = [psa(0, THW), psa(1024, min(2048, 1024 + THW)), psb(0, THW)]
        ri = 0
        for j, (th, do_, di) in enumerate(chunks):
            ws = j % 4
            THL = min(THW, S - th * THW)
            if di == 0:
                pa, pab = rot[ri % 3]
                ri += 1

            def mm_qkv(e, di=di, ws=ws, pa=pa, th=th, THL=THL):
                last = None
                for pi, (wop, zop) in enumerate(((w1hi, bigA_hi), (w1hi, bigA_lo), (w1lo, bigA_hi))):
                    for nt in range(cdiv(THL, 512)):
                        n0 = th * THW + nt * 512
                        n1 = th * THW + min(THL, (nt + 1) * 512)
                        last = e.matmul(out=pa[:, n0 - th * THW:n1 - th * THW],
                                        lhsT=wop[:, ws * 128:(ws + 1) * 128],
                                        rhs=zop[:, di * S + n0:di * S + n1],
                                        start=(di == 0 and pi == 0), stop=(di == NC - 1 and pi == 2))
                return last
            p.do("tensor", mm_qkv, reads=[gb("w1hi%d" % ws), gb("w1lo%d" % ws), zTh, zTl], writes=pab, inc="t")
            if j + 4 < len(chunks):
                qkv_load(j + 4)
            if di != NC - 1:
                continue
            if th == 0:
                pop_pend(1)
            if do_ == 0 and th > 0:
                while vpend:
                    vpend.pop(0)()
            while len(vpend) >= 2:
                vpend.pop(0)()
            hh, hl, hhb, hlb = ((ht_hi, ht_lo, gb("ht_hi"), gb("ht_lo")),
                                (ht2_hi, ht2_lo, gb("ht2_hi"), gb("ht2_lo")))[do_ % 2]
            if do_ < 2 * NC:
                dst_hi, dst_lo = (q_hi_h, q_lo_h) if do_ < NC else (k_hi_h, k_lo_h)
                row = (do_ % NC) * 128
                split_hilo(pa[:, 0:THL], pab, hh[:, 0:THL], hhb, hl[:, 0:THL], hlb)
                dma_out(dst_hi[row:row + 128, th * THW:th * THW + THL], hh[:, 0:THL], reads=[hhb],
                        writes=[gb("q_h") if do_ < NC else gb("k_h")])
                dma_out(dst_lo[row:row + 128, th * THW:th * THW + THL], hl[:, 0:THL], reads=[hlb],
                        writes=[gb("q_h") if do_ < NC else gb("k_h")])
            else:
                dv = do_ - 2 * NC
                split_hilo(pa[:, 0:THL], pab, hh[:, 0:THL], hhb, hl[:, 0:THL], hlb)

                def pend_v(dv=dv, hh=hh, hl=hl, hhb=hhb, hlb=hlb, th=th, THL=THL):
                    TPH_ = THL // 128
                    for hb_, hbb, voh in ((hh, hhb, v_hi_h), (hl, hlb, v_lo_h)):
                        for b0 in range(0, TPH_, 8):
                            nb = min(8, TPH_ - b0)
                            def tr8(e, hb_=hb_, b0=b0, nb=nb):
                                last = None
                                for t_ in range(nb):
                                    last = e.transpose(out=psT[:, t_ * 128:(t_ + 1) * 128],
                                                       in_=hb_[:, (b0 + t_) * 128:(b0 + t_ + 1) * 128],
                                                       identity=identb[:])
                                return last
                            p.do("tensor", tr8, reads=[hbb, gb("identb")], writes=[PB[7]], inc="t")
                            p.do("vector", lambda e, hb_=hb_, b0=b0, nb=nb: e.tensor_copy(
                                out=hb_[:, b0 * 128:(b0 + nb) * 128], in_=psT[:, 0:nb * 128]),
                                reads=[PB[7]], writes=[hbb], inc="v")
                        vo = bass.AP(voh.tensor, (th * THW) * D + dv * 128,
                                     [[D, 128], [128 * D, TPH_], [1, 128]])
                        dma_out(vo, hb_[:, 0:THL], reads=[hbb], writes=[gb("v_h")])
                vpend.append(pend_v)
        while vpend:
            vpend.pop(0)()
        pop_pend(99)
        p.flush_block()

    def scores_phase():
        p.do("gpsimd", lambda e: e.memset(zbf[:], 0.0), writes=[gb("zbf")], inc="g")

        TPG = QG // 128
        for i in range(NT):
            gmax = (i // TPG) * TPG + TPG - 1
            for kc in range(i + 1, min(gmax, NT - 1) + 1):
                for dsth in (pt_hi_h, pt_lo_h):
                    dma_out(dsth[kc * 128:(kc + 1) * 128, i * 128:(i + 1) * 128], zbf[:],
                            reads=[gb("zbf")], writes=[gb("pt_h")])
        KT0 = min(512, S)
        Kearly, Klate = gb("K_early"), gb("K_late")
        for dc in range(NC):
            dma_in(bigA_hi[:, dc * S:dc * S + KT0], k_hi_h[dc * 128:(dc + 1) * 128, 0:KT0],
                   writes=[Kearly], reads=[gb("k_h")])
            dma_in(bigA_lo[:, dc * S:dc * S + KT0], k_lo_h[dc * 128:(dc + 1) * 128, 0:KT0],
                   writes=[Kearly], reads=[gb("k_h")])
        if S > KT0:
            for dc in range(NC):
                dma_in(bigA_hi[:, dc * S + KT0:(dc + 1) * S], k_hi_h[dc * 128:(dc + 1) * 128, KT0:],
                       writes=[Klate], reads=[gb("k_h")])
                dma_in(bigA_lo[:, dc * S + KT0:(dc + 1) * S], k_lo_h[dc * 128:(dc + 1) * 128, KT0:],
                       writes=[Klate], reads=[gb("k_h")])
        rot = [psa(0, 1024), psa(1024, 2048), psb(0, 1024)]
        ri = 0
        pend = []

        def emit_pend():
            while pend:
                pend.pop(0)()

        for i in range(NT):
            keys = (i + 1) * 128
            qh = gb("qt_hi%d" % (i % 2))
            ql = gb("qt_lo%d" % (i % 2))
            QW = NC * 128
            qhs = sv_hi[:, (i % 2) * QW:(i % 2) * QW + QW]
            qls = sv_lo[:, (i % 2) * QW:(i % 2) * QW + QW]
            for half, (hbuf, dsth) in enumerate(((qhs, q_hi_h), (qls, q_lo_h))):
                for quarter in range(4):
                    c0 = quarter * (NC // 4) if NC >= 4 else 0
                    c1 = (quarter + 1) * (NC // 4) if NC >= 4 else NC
                    if NC < 4 and quarter > 0:
                        continue
                    ap_in = bass.AP(dsth.tensor, (c0 * 128) * S + i * 128,
                                    [[S, 128], [128 * S, c1 - c0], [1, 128]])
                    dma_in_hw(hbuf[:, c0 * 128:c1 * 128], ap_in, writes=[qh if half == 0 else ql], reads=[gb("q_h")])
            nseg = cdiv(keys, 1024)
            segbufs = []
            for sg in range(nseg):
                s0, s1 = sg * 1024, min(keys, (sg + 1) * 1024)
                pa, pab = rot[ri % 3]
                ri += 1
                segbufs.append((s0, s1, pa, pab))

                def mm_sc1(e, i=i, s0=s0, s1=s1, pa=pa, qhs=qhs):
                    last = None
                    for dc in range(NC):
                        for nt in range(cdiv(s1 - s0, 512)):
                            n0 = s0 + nt * 512
                            n1 = min(s1, n0 + 512)
                            last = e.matmul(out=pa[:, n0 - s0:n1 - s0],
                                            lhsT=qhs[:, dc * 128:(dc + 1) * 128],
                                            rhs=bigA_hi[:, dc * S + n0:dc * S + n1],
                                            start=(dc == 0), stop=False)
                    return last

                def mm_sc23(e, i=i, s0=s0, s1=s1, pa=pa, qhs=qhs, qls=qls):
                    last = None
                    for pi, (qop, kop) in enumerate(((qhs, bigA_lo), (qls, bigA_hi))):
                        for dc in range(NC):
                            for nt in range(cdiv(s1 - s0, 512)):
                                n0 = s0 + nt * 512
                                n1 = min(s1, n0 + 512)
                                last = e.matmul(out=pa[:, n0 - s0:n1 - s0],
                                                lhsT=qop[:, dc * 128:(dc + 1) * 128],
                                                rhs=kop[:, dc * S + n0:dc * S + n1],
                                                start=False, stop=(pi == 1 and dc == NC - 1))
                    return last
                kreads = [Kearly] + ([Klate] if s1 > KT0 else [])
                p.do("tensor", mm_sc1, reads=[qh] + kreads, writes=pab, inc="t")
                p.do("tensor", mm_sc23, reads=[qh, ql] + kreads, writes=pab, inc="t")

            hh, hhb = (ht_hi, gb("ht_hi")) if i % 2 == 0 else (ht2_hi, gb("ht2_hi"))
            hl, hlb = (ht_lo, gb("ht_lo")) if i % 2 == 0 else (ht2_lo, gb("ht2_lo"))
            seeded = False
            for si, (s0, s1, pa, pab) in enumerate(segbufs):
                dlen = 128 if s1 == keys else 0
                if s1 - dlen > s0:
                    seed = nmax[:] if seeded else -1e30
                    p.do("vector", lambda e, pa=pa, s0=s0, s1=s1, dlen=dlen, seed=seed: e.tensor_scalar(
                        out=ytile[:, s0:s1 - dlen], in0=pa[:, 0:s1 - dlen - s0], scalar1=smax_scale,
                        scalar2=seed, op0=ALU.mult, op1=ALU.max, accum_out=nmax[:]),
                        reads=pab + [gb("nmax")], writes=[gb("ytile"), gb("nmax")], inc="v")
                    seeded = True
                if dlen:
                    p.do("vector", lambda e, pa=pa, s0=s0, s1=s1: e.scalar_tensor_tensor(
                        out=ytile[:, s1 - 128:s1], in0=pa[:, s1 - 128 - s0:s1 - s0], scalar=smax_scale,
                        in1=maskt[:], op0=ALU.mult, op1=ALU.add),
                        reads=pab + [gb("maskt")], writes=[gb("ytile")], inc="v")
                    seed = nmax[:] if seeded else -1e30
                    p.do("vector", lambda e, s1=s1, seed=seed: e.tensor_scalar(
                        out=ttile[:, 0:128], in0=ytile[:, s1 - 128:s1], scalar1=1.0,
                        scalar2=seed, op0=ALU.mult, op1=ALU.max, accum_out=nmax[:]),
                        reads=[gb("ytile"), gb("nmax")], writes=[gb("ttile"), gb("nmax")], inc="v")
                    seeded = True
            p.do("vector", lambda e: e.tensor_scalar(out=nmax[:], in0=nmax[:], scalar1=-1.0, scalar2=None,
                                                     op0=ALU.mult),
                 writes=[gb("nmax")], inc="v")
            p.do("scalar", lambda e, keys=keys: e.activation(out=albc[:, 0:keys], in_=ytile[:, 0:keys],
                                                             func=AF.Exp, bias=nmax[:], scale=1.0,
                                                             accum_out=lsum[:]),
                 reads=[gb("ytile"), gb("nmax")], writes=[gb("albc"), gb("lsum")], inc="s")
            p.do("vector", lambda e: e.reciprocal(out=rinv[:], in_=lsum[:]),
                 reads=[gb("lsum")], writes=[gb("rinv")], inc="v")
            p.do("vector", lambda e, keys=keys, hh=hh: e.tensor_scalar(
                out=hh[:, 0:keys], in0=albc[:, 0:keys], scalar1=rinv[:], scalar2=None, op0=ALU.mult),
                reads=[gb("albc"), gb("rinv")], writes=[hhb], inc="v")
            p.do("vector", lambda e, keys=keys, hh=hh, hl=hl: e.scalar_tensor_tensor(
                out=hl[:, 0:keys], in0=albc[:, 0:keys], scalar=rinv[:], in1=hh[:, 0:keys],
                op0=ALU.mult, op1=ALU.subtract),
                reads=[gb("albc"), gb("rinv"), hhb], writes=[hlb], inc="v")

            def tr_pt(i=i, hh=hh, hl=hl, hhb=hhb, hlb=hlb):
                for b0 in range(0, i + 1, 8):
                    nb = min(8, i + 1 - b0)
                    for hb_, hbb, dsth in ((hh, hhb, pt_hi_h), (hl, hlb, pt_lo_h)):
                        def tr8(e, hb_=hb_, b0=b0, nb=nb):
                            last = None
                            for j in range(nb):
                                last = e.transpose(out=psT[:, j * 128:(j + 1) * 128],
                                                   in_=hb_[:, (b0 + j) * 128:(b0 + j + 1) * 128],
                                                   identity=identb[:])
                            return last
                        p.do("tensor", tr8, reads=[hbb, gb("identb")], writes=[PB[7]], inc="t")
                        p.do("vector", lambda e, hb_=hb_, b0=b0, nb=nb: e.tensor_copy(
                            out=hb_[:, b0 * 128:(b0 + nb) * 128], in_=psT[:, 0:nb * 128]),
                            reads=[PB[7]], writes=[hbb], inc="v")
                        dst = bass.AP(dsth.tensor, b0 * 128 * S + i * 128,
                                      [[S, 128], [128 * S, nb], [1, 128]])
                        dma_out(dst, hb_[:, b0 * 128:(b0 + nb) * 128], reads=[hbb], writes=[gb("pt_h")])
            pend.append(tr_pt)
            if len(pend) > 1:
                pend.pop(0)()
        emit_pend()
        p.flush_block()

    def av_phase():
        vhi_bufs = [gb("Vh_kc%d" % kc) for kc in range(NT)]
        vlo_bufs = [gb("Vl_kc%d" % kc) for kc in range(NT)]
        SLOTS = min(16, (2 * S) // QG + (2 * D) // QG)
        def slot_ap(hi, s_):
            base = (sv_hi, sv_lo)[0 if hi else 1]
            n_sv = (2 * S) // QG
            if s_ < n_sv:
                return (sv_hi if hi else sv_lo)[:, s_ * QG:(s_ + 1) * QG]
            s2 = s_ - n_sv
            n_ht = D // QG
            if s2 < n_ht:
                return (ht_hi if hi else ht_lo)[:, s2 * QG:(s2 + 1) * QG]
            s3 = s2 - n_ht
            return (ht2_hi if hi else ht2_lo)[:, s3 * QG:(s3 + 1) * QG]
        slot_buf = [gb("ptsl%d" % s_) for s_ in range(SLOTS)]
        slot_seq = [0]
        slot_of = {}

        def load_pt(g):
            nkc = (g * QG + QG) // 128
            for kc in range(nkc):
                s_ = slot_seq[0] % SLOTS
                slot_seq[0] += 1
                slot_of[(g, kc)] = s_
                dma_in_hw(slot_ap(True, s_),
                          pt_hi_h[kc * 128:(kc + 1) * 128, g * QG:(g + 1) * QG],
                          writes=[slot_buf[s_]], reads=[gb("pt_h")])
                dma_in_hw(slot_ap(False, s_),
                          pt_lo_h[kc * 128:(kc + 1) * 128, g * QG:(g + 1) * QG],
                          writes=[slot_buf[s_]], reads=[gb("pt_h")])

        load_pt(0)
        for kc in range(NT):
            dma_in(bigA_hi[:, kc * D:(kc + 1) * D], v_hi_h[kc * 128:(kc + 1) * 128, :],
                   writes=[vhi_bufs[kc]], reads=[gb("v_h")])
        for kc in range(NT):
            dma_in(bigA_lo[:, kc * D:(kc + 1) * D], v_lo_h[kc * 128:(kc + 1) * 128, :],
                   writes=[vlo_bufs[kc]], reads=[gb("v_h")])
        for g in range(NQG):
            nkc = (g * QG + QG) // 128
            myslots = [slot_buf[slot_of[(g, kc)]] for kc in range(nkc)]
            for dv in range(NC):
                pa = psA[:, (dv % 4) * 512:(dv % 4) * 512 + QG]
                pab = [PB[dv % 4]] if QG <= 512 else None

                def mm_av12(e, g=g, dv=dv, nkc=nkc, pa=pa):
                    last = None
                    for pi in range(2):
                        for kc in range(nkc):
                            s_ = slot_of[(g, kc)]
                            last = e.matmul(out=pa, lhsT=bigA_hi[:, kc * D + dv * 128:kc * D + (dv + 1) * 128],
                                            rhs=slot_ap(pi == 0, s_),
                                            start=(pi == 0 and kc == 0), stop=False)
                    return last

                def mm_av3(e, g=g, dv=dv, nkc=nkc, pa=pa):
                    last = None
                    for kc in range(nkc):
                        s_ = slot_of[(g, kc)]
                        last = e.matmul(out=pa, lhsT=bigA_lo[:, kc * D + dv * 128:kc * D + (dv + 1) * 128],
                                        rhs=slot_ap(True, s_),
                                        start=False, stop=(kc == nkc - 1))
                    return last
                p.do("tensor", mm_av12, reads=vhi_bufs[:nkc] + myslots, writes=pab, inc="t")
                p.do("tensor", mm_av3, reads=vlo_bufs[:nkc] + myslots, writes=pab, inc="t")
                nsa = max(1, S // QG)
                asl = albc[:, (dv % nsa) * QG:(dv % nsa) * QG + QG]
                p.do("scalar", lambda e, pa=pa, asl=asl: e.activation(out=asl, in_=pa, func=AF.Identity),
                     reads=pab, writes=[gb("albc")], inc="s")
                dma_out(aot_f_h[dv * 128:(dv + 1) * 128, g * QG:(g + 1) * QG], asl,
                        reads=[gb("albc")], writes=[gb("aot_h")])
            if g + 1 < NQG:
                load_pt(g + 1)
        p.flush_block()

    def oproj_phase():
        ochunk = [gb("O_dv%d" % dv) for dv in range(NC)]

        def load_tile(i):
            af = gb("aot_f%d" % (i % 2))
            QW = NC * 128
            fb = (ytile, albc)[i % 2]
            fbb = gb(("ytile", "albc")[i % 2])
            for quarter in range(4):
                c0 = quarter * (NC // 4) if NC >= 4 else 0
                c1 = (quarter + 1) * (NC // 4) if NC >= 4 else NC
                if NC < 4 and quarter > 0:
                    continue
                ap_in = bass.AP(aot_f_h.tensor, (c0 * 128) * S + i * 128,
                                [[S, 128], [128 * S, c1 - c0], [1, 128]])
                dma_in(fb[:, c0 * 128:c1 * 128], ap_in, writes=[fbb], reads=[gb("aot_h")])
            xa = (xa0, xa1)[i % 2]
            xab = gb("xa%d" % (i % 2))
            dma_in(xa[:], xp_h[i * 128:(i + 1) * 128, :], writes=[xab], reads=[gb("xp_h")])
            return af, fb, fbb, i, xa, xab

        def split_tile(af, fb, fbb, i):
            ah = gb("aot_hi%d" % (i % 2))
            al = gb("aot_lo%d" % (i % 2))
            QW = NC * 128
            ahs = sv_hi[:, (i % 2) * QW:(i % 2) * QW + QW]
            als = sv_lo[:, (i % 2) * QW:(i % 2) * QW + QW]
            split_hilo(fb[:, 0:QW], [fbb], ahs, ah, als, al)
            return ah, al, ahs, als

        t_loaded = load_tile(0)
        for dv in range(NC):
            fb = ttile
            fbb = gb("ttile")
            dma_in(fb[:, 0:D], oproj_h[dv * 128:(dv + 1) * 128, :], writes=[fbb])
            split_hilo(fb[:, 0:D], [fbb], bigA_hi[:, dv * D:(dv + 1) * D], ochunk[dv],
                       bigA_lo[:, dv * D:(dv + 1) * D], ochunk[dv])
        rot = [psa(0, 1024), psa(1024, 2048), psb(0, 1024)]
        ri = 0
        for i in range(NT):
            af, fb, fbb, i_, xa, xab = t_loaded
            ah, al, ahs, als = split_tile(af, fb, fbb, i_)
            if i + 1 < NT:
                t_next = load_tile(i + 1)
            ob = (ytile, ttile)[i % 2]
            obb = gb(("ytile", "ttile")[i % 2])
            DH = min(D, 1024)
            for dh in range(cdiv(D, DH)):
                pa, pab = rot[ri % 3]
                ri += 1
                DW = min(DH, D - dh * DH)

                for dv in range(NC):
                    def mm_o(e, dh=dh, pa=pa, ahs=ahs, als=als, DH=DH, DW=DW, dv=dv):
                        last = None
                        for pi, (aop, oop) in enumerate(((ahs, bigA_hi), (ahs, bigA_lo), (als, bigA_hi))):
                            for nt in range(cdiv(DW, 512)):
                                n0, n1 = nt * 512, min(DW, (nt + 1) * 512)
                                last = e.matmul(out=pa[:, n0:n1], lhsT=aop[:, dv * 128:(dv + 1) * 128],
                                                rhs=oop[:, dv * D + dh * DH + n0:dv * D + dh * DH + n1],
                                                start=(pi == 0 and dv == 0), stop=(pi == 2 and dv == NC - 1))
                        return last
                    p.do("tensor", mm_o, reads=[ah, al, ochunk[dv]], writes=pab, inc="t")
                p.do("vector", lambda e, pa=pa, dh=dh, xa=xa, ob=ob, DH=DH, DW=DW: e.tensor_tensor(
                    out=ob[:, dh * DH:dh * DH + DW], in0=pa[:, 0:DW], in1=xa[:, dh * DH:dh * DH + DW], op=ALU.add),
                    reads=pab + [xab], writes=[obb], inc="v")
            dma_out(x2_h[i * 128:(i + 1) * 128, :], ob[:], reads=[obb], writes=[gb("x2_h")])
            if i + 1 < NT:
                t_loaded = t_next
        p.flush_block()

    prep_consts()
    if "p1" in phases and cut >= 1:
        prep_vu()
        if cut >= 2:
            ffn_phase(x_h, x1_h, gb("x_in_h"), gb("x1_h"), "x1t")
        else:
            p.flush_block()
    if "p2" in phases:
        proj_phase()
    if "p3" in phases:
        scores_phase()
    if "p4" in phases:
        av_phase()
    if "p5" in phases:
        oproj_phase()
    if "p6" in phases:
        prep_vu()
        ffn_phase(x2_h, out_h, gb("x2_h"), gb("out_hb"), "outt")
    fin = [(s, p.cnt[s], 16) for s in p.cnt if s.startswith("stB_")]
    p.do("gpsimd", lambda e: e.nop(), extra=fin)
    p.flush_block()
    nc.finalize()
    return nc


def np_softmax(x, axis=-1):
    m = x.max(axis=axis, keepdims=True)
    e = np.exp(x - m)
    return e / e.sum(axis=axis, keepdims=True)


def ref_ln(x, eps=1e-5):
    m = x.mean(-1, keepdims=True)
    v = ((x - m) ** 2).mean(-1, keepdims=True)
    return (x - m) / np.sqrt(v + eps)


def ref_ffn(y, U, V, W1, b1, W2, b2):
    h = np.maximum(y @ W1 + b1, 0)
    a = np_softmax(h @ W2 + b2)
    vx = np.einsum('krd,sd->skr', V, y)
    uvx = np.einsum('kdr,skr->skd', U, vx)
    return np.einsum('sk,skd->sd', a, uvx)


def ref_attn(xn, qkv, o_proj, gamma, beta, eps=1e-5):
    D = xn.shape[-1]
    z = ref_ln(xn, eps) * gamma + beta
    q, k, v = z @ qkv[:, :D], z @ qkv[:, D:2 * D], z @ qkv[:, 2 * D:]
    sc = (q @ k.T) / np.sqrt(np.float32(D))
    S = xn.shape[0]
    sc = np.where(np.tril(np.ones((S, S), dtype=bool)), sc, -np.inf)
    return np_softmax(sc) @ v @ o_proj + xn


def ref_full(x, qkv, o_proj, gamma, beta, U, V, W1, b1, W2, b2, upto=3):
    x = x + 0.5 * ref_ffn(ref_ln(x), U, V, W1, b1, W2, b2)
    if upto >= 2:
        x = x + ref_attn(ref_ln(x), qkv, o_proj, gamma, beta)
    if upto >= 3:
        x = x + 0.5 * ref_ffn(ref_ln(x), U, V, W1, b1, W2, b2)
    return x


def mk_inputs(S, D, R, K, seed=0):
    rng = np.random.default_rng(seed)
    f = lambda *sh: rng.standard_normal(sh).astype(np.float32)
    bq = 1.0 / np.sqrt(D)
    return dict(
        x=f(S, D), qkv=rng.uniform(-bq, bq, (D, 3 * D)).astype(np.float32),
        o_proj=rng.uniform(-bq, bq, (D, D)).astype(np.float32),
        gamma=np.ones(D, np.float32), beta=np.zeros(D, np.float32),
        U=f(K, D, R) * 0.02, V=f(K, R, D) * 0.02,
        W1=f(D, D) * 0.02, b1=np.zeros(D, np.float32),
        W2=f(D, K) * 0.02, b2=np.zeros(K, np.float32),
    )
_CACHED = {}


def _get_nc():
    if "nc" not in _CACHED:
        _CACHED["nc"] = build(dict(S=2048, D=2048, R=64, K=8))
    return _CACHED["nc"]


def kernel(L0_attn_x, L0_attn_qkv, L0_attn_o_proj, L0_attn_gamma, L0_attn_beta,
           L0_ffn_U, L0_ffn_V, L0_ffn_W1, L0_ffn_b1, L0_ffn_W2, L0_ffn_b2):
    from concourse.bass_utils import run_bass_kernel_spmd
    nc = _get_nc()
    w = dict(
        qkv=np.ascontiguousarray(L0_attn_qkv, dtype=np.float32),
        o_proj=np.ascontiguousarray(L0_attn_o_proj, dtype=np.float32),
        gamma=np.ascontiguousarray(L0_attn_gamma, dtype=np.float32),
        beta=np.ascontiguousarray(L0_attn_beta, dtype=np.float32),
        U=np.ascontiguousarray(L0_ffn_U, dtype=np.float32),
        V=np.ascontiguousarray(L0_ffn_V, dtype=np.float32),
        W1=np.ascontiguousarray(L0_ffn_W1, dtype=np.float32),
        b1=np.ascontiguousarray(L0_ffn_b1, dtype=np.float32),
        W2=np.ascontiguousarray(L0_ffn_W2, dtype=np.float32),
        b2=np.ascontiguousarray(L0_ffn_b2, dtype=np.float32),
    )
    B = L0_attn_x.shape[0]
    in_maps = [dict(w, x=np.ascontiguousarray(L0_attn_x[b], dtype=np.float32)) for b in range(B)]
    res = run_bass_kernel_spmd(nc, in_maps, core_ids=list(range(B)))
    return np.stack([res.results[b]["out"] for b in range(B)], axis=0)


# version 16
# speedup vs baseline: 1.0152x; 1.0152x over previous
import numpy as np
import concourse.bass as bass
import concourse.mybir as mybir

F32 = mybir.dt.float32
BF16 = mybir.dt.bfloat16
AF = mybir.ActivationFunctionType
ALU = mybir.AluOpType
AX = mybir.AxisListType

ENGINES = ("sync", "scalar", "vector", "tensor", "gpsimd")


def cdiv(a, b):
    return (a + b - 1) // b


class Buf:
    __slots__ = ("name", "w", "r")

    def __init__(self, name):
        self.name = name
        self.w = {}
        self.r = {}

    def reset(self):
        self.w = {}
        self.r = {}


def _acc(d, ev):
    if ev is not None:
        s, v = ev[0], ev[1]
        if d.get(s, 0) < v:
            d[s] = v


class Prog:
    def __init__(self, nc):
        self.nc = nc
        self.sems = {}
        self.cnt = {}
        self.ops = {e: [] for e in ENGINES}
        self.waited = {e: {} for e in ENGINES}
        self.bufs = {}
        self.nops = 0

    def sem(self, name):
        if name not in self.sems:
            self.sems[name] = self.nc.alloc_semaphore(name)
            self.cnt[name] = 0
        return self.sems[name]

    def buf(self, name):
        if name not in self.bufs:
            self.bufs[name] = Buf(name)
        return self.bufs[name]

    def do(self, eng, fn, reads=(), writes=(), extra=(), inc=None, n=1):
        assert inc is not None or (not reads and not writes)
        deps = {}
        for b in reads:
            for s, v in b.w.items():
                if deps.get(s, 0) < v:
                    deps[s] = v
        for b in writes:
            for d in (b.w, b.r):
                for s, v in d.items():
                    if deps.get(s, 0) < v:
                        deps[s] = v
        for d in extra:
            if d is not None and deps.get(d[0], 0) < d[1]:
                deps[d[0]] = d[1]
        ev = None
        if inc is not None:
            self.sem(inc)
            self.cnt[inc] += n
            ev = (inc, self.cnt[inc], n)
        self.ops[eng].append((fn, deps, ev))
        self.nops += 1
        for b in reads:
            _acc(b.r, ev)
        for b in writes:
            _acc(b.w, ev)
        return ev

    def flush_block(self):
        nc = self.nc
        with nc.Block() as blk:
            for ename in ENGINES:
                ops = self.ops[ename]
                if not ops:
                    continue

                def emitter(eng, _ops=ops, _ename=ename):
                    for fn, deps, ev in _ops:
                        for sname in sorted(deps):
                            val = deps[sname]
                            if _ename == "tensor" and sname == "t":
                                continue
                            if self.waited[_ename].get(sname, 0) >= val:
                                continue
                            eng.wait_ge(self.sems[sname], val)
                            self.waited[_ename][sname] = val
                        instr = fn(eng)
                        if ev is not None:
                            instr.then_inc(self.sems[ev[0]], ev[2])
                getattr(blk, ename)(emitter)
        self.ops = {e: [] for e in ENGINES}


def build(cfg):
    S = cfg["S"]
    D = cfg["D"]
    R = cfg["R"]
    K = cfg["K"]
    phases = cfg.get("phases", ("p1", "p2", "p3", "p4", "p5", "p6"))
    cut = cfg.get("cut", 99)
    last_phase = phases[-1]

    NT = S // 128
    NC = D // 128
    NSEG = cdiv(D, 512)
    G1 = 2
    TG = S // G1
    TGT = TG // 128
    NPAIR = K // 2
    assert 2 * R == 128 and TG <= 1024
    QG = min(512, S)
    NQG = S // QG

    eps = 1e-5
    smax_scale = float(np.float32(1.0) / np.sqrt(np.float32(D)))

    nc = bass.Bass("TRN2", target_bir_lowering=False, debug=False)
    p = Prog(nc)

    x_h = nc.dram_tensor("x", [S, D], F32, kind="ExternalInput").ap()
    qkv_h = nc.dram_tensor("qkv", [D, 3 * D], F32, kind="ExternalInput").ap()
    oproj_h = nc.dram_tensor("o_proj", [D, D], F32, kind="ExternalInput").ap()
    gamma_h = nc.dram_tensor("gamma", [D], F32, kind="ExternalInput").ap()
    beta_h = nc.dram_tensor("beta", [D], F32, kind="ExternalInput").ap()
    U_h = nc.dram_tensor("U", [K, D, R], F32, kind="ExternalInput").ap()
    V_h = nc.dram_tensor("V", [K, R, D], F32, kind="ExternalInput").ap()
    W1_h = nc.dram_tensor("W1", [D, D], F32, kind="ExternalInput").ap()
    b1_h = nc.dram_tensor("b1", [D], F32, kind="ExternalInput").ap()
    W2_h = nc.dram_tensor("W2", [D, K], F32, kind="ExternalInput").ap()
    b2_h = nc.dram_tensor("b2", [K], F32, kind="ExternalInput").ap()
    out_h = nc.dram_tensor("out", [S, D], F32, kind="ExternalOutput").ap()

    x1_h = nc.dram_tensor("x1_scratch", [S, D], F32).ap()
    xp_h = nc.dram_tensor("xp_scratch", [S, D], F32).ap()
    x2_h = nc.dram_tensor("x2_scratch", [S, D], F32).ap()
    q_hi_h = nc.dram_tensor("q_hi", [D, S], BF16).ap()
    q_lo_h = nc.dram_tensor("q_lo", [D, S], BF16).ap()
    k_hi_h = nc.dram_tensor("k_hi", [D, S], BF16).ap()
    k_lo_h = nc.dram_tensor("k_lo", [D, S], BF16).ap()
    v_hi_h = nc.dram_tensor("v_hi", [S, D], BF16).ap()
    v_lo_h = nc.dram_tensor("v_lo", [S, D], BF16).ap()
    pt_hi_h = nc.dram_tensor("pt_hi", [S, S], BF16).ap()
    pt_lo_h = nc.dram_tensor("pt_lo", [S, S], BF16).ap()
    aot_f_h = nc.dram_tensor("aot_f", [D, S], F32).ap()
    alpha_hb = nc.dram_tensor("alpha_scratch", [K, 1024], F32)
    alpha_h = alpha_hb.ap()
    lmax_hb = nc.dram_tensor("lmax_scratch", [1, 1024], F32)
    lmax_h = lmax_hb.ap()

    BIGW = max(NC * S, NC * (S // G1) + (K // 2) * NC * 128 + (K // 2) * D)
    bigA_hi = nc.alloc_sbuf_tensor("bigA_hi", [128, BIGW], BF16)
    bigA_lo = nc.alloc_sbuf_tensor("bigA_lo", [128, BIGW], BF16)
    xa0 = nc.alloc_sbuf_tensor("xa0", [128, D], F32)
    xa1 = nc.alloc_sbuf_tensor("xa1", [128, D], F32)
    ytile = nc.alloc_sbuf_tensor("ytile", [128, D], F32)
    ttile = nc.alloc_sbuf_tensor("ttile", [128, D], F32)
    ht_hi = nc.alloc_sbuf_tensor("ht_hi", [128, D], BF16)
    ht_lo = nc.alloc_sbuf_tensor("ht_lo", [128, D], BF16)
    ht2_hi = nc.alloc_sbuf_tensor("ht2_hi", [128, D], BF16)
    ht2_lo = nc.alloc_sbuf_tensor("ht2_lo", [128, D], BF16)
    sv_hi = nc.alloc_sbuf_tensor("sv_hi", [128, 2 * S], BF16)
    sv_lo = nc.alloc_sbuf_tensor("sv_lo", [128, 2 * S], BF16)
    albc = nc.alloc_sbuf_tensor("albc", [128, S], F32)

    w1f = nc.alloc_sbuf_tensor("w1f", [128, 512], F32)
    w1hi = nc.alloc_sbuf_tensor("w1hi", [128, 512], BF16)
    w1lo = nc.alloc_sbuf_tensor("w1lo", [128, 512], BF16)
    w2hi = nc.alloc_sbuf_tensor("w2hi", [128, NC * K], BF16)
    w2lo = nc.alloc_sbuf_tensor("w2lo", [128, NC * K], BF16)
    b1sb = nc.alloc_sbuf_tensor("b1sb", [128, NC], F32)
    gsb = nc.alloc_sbuf_tensor("gsb", [128, NC], F32)
    bsb = nc.alloc_sbuf_tensor("bsb", [128, NC], F32)
    b2sb = nc.alloc_sbuf_tensor("b2sb", [128, 1], F32)
    ident = nc.alloc_sbuf_tensor("ident", [128, 128], F32)
    identb = nc.alloc_sbuf_tensor("identb", [128, 128], BF16)
    maskt = nc.alloc_sbuf_tensor("maskt", [128, 128], F32)
    stats = nc.alloc_sbuf_tensor("stats", [128, 6 * NSEG], F32)
    stats2 = nc.alloc_sbuf_tensor("stats2", [128, 6 * NSEG], F32)
    mvt = nc.alloc_sbuf_tensor("mvt", [128, 2], F32)
    sdt = nc.alloc_sbuf_tensor("sdt", [128, 1], F32)
    rstd = nc.alloc_sbuf_tensor("rstd", [128, 1], F32)
    mvt2 = nc.alloc_sbuf_tensor("mvt2", [128, 2], F32)
    sdt2 = nc.alloc_sbuf_tensor("sdt2", [128, 1], F32)
    rstd2 = nc.alloc_sbuf_tensor("rstd2", [128, 1], F32)
    nmax = nc.alloc_sbuf_tensor("nmax", [128, 1], F32)
    lsum = nc.alloc_sbuf_tensor("lsum", [128, 1], F32)
    rinv = nc.alloc_sbuf_tensor("rinv", [128, 1], F32)
    RW = min(1024, S // 2)
    assert TG <= RW
    logt = albc[0:K, RW:2 * RW]
    alphaT = logt
    lmax = albc[0:1, 0:RW]
    lmaxB = albc[0:K, 0:RW]
    ztmp = nc.alloc_sbuf_tensor("ztmp", [128, 128], F32)
    epssb = nc.alloc_sbuf_tensor("epssb", [128, 1], F32)
    ptb = nc.alloc_sbuf_tensor("ptb", [128, 256], BF16)
    zbf = nc.alloc_sbuf_tensor("zbf", [128, 128], BF16)
    ones8m = nc.alloc_sbuf_tensor("ones8m", [K, 1], F32)
    ones8s = nc.alloc_sbuf_tensor("ones8s", [K, 1], F32)

    psA = nc.alloc_psum_tensor("psA", [128, 2048], F32)
    psB = nc.alloc_psum_tensor("psB", [128, 1024], F32)
    psC = nc.alloc_psum_tensor("psC", [128, 512], F32)
    psT = nc.alloc_psum_tensor("psT", [128, 1024], BF16)
    PB = [p.buf("PB%d" % i) for i in range(8)]

    def pbanks(base, c0, c1):
        return PB[base + c0 // 512: base + cdiv(c1, 512)]

    def psa(c0, c1):
        return psA[:, c0:c1], pbanks(0, c0, c1)

    def psb(c0, c1):
        return psB[:, c0:c1], pbanks(4, c0, c1)

    def psc(c0, c1):
        return psC[:, c0:c1], pbanks(6, c0, c1)

    gb = p.buf

    def dma_in(dst_ap, src_ap, writes, reads=(), slow=False):
        sem = "ldB_" + writes[0].name
        if slow:
            return p.do("sync", lambda e, d=dst_ap, s=src_ap: e.dma_start(out=d, in_=s, allow_slow_non_contiguous=True),
                        reads=reads, writes=writes, inc=sem, n=16)
        return p.do("sync", lambda e, d=dst_ap, s=src_ap: e.dma_start(out=d, in_=s),
                    reads=reads, writes=writes, inc=sem, n=16)

    def dma_out(dst_ap, src_ap, reads, writes=()):
        sem = "stB_" + reads[0].name
        return p.do("gpsimd", lambda e, d=dst_ap, s=src_ap: e.dma_start(out=d, in_=s),
                    reads=reads, writes=writes, inc=sem, n=16)

    def dma_acc(dst_ap, src_ap, reads, writes=()):
        sem = "stB_" + reads[0].name
        return p.do("gpsimd", lambda e, d=dst_ap, s=src_ap: e.dma_start(out=d, in_=s, accum_op=ALU.add),
                    reads=reads, writes=writes, inc=sem, n=16)

    def dma_out_hw(dst_ap, src_ap, reads, writes=()):
        sem = "shB_" + reads[0].name
        return p.do("scalar", lambda e, d=dst_ap, s=src_ap: e.dma_start(out=d, in_=s),
                    reads=reads, writes=writes, inc=sem, n=16)

    def dma_in_hw(dst_ap, src_ap, writes, reads=()):
        sem = "lhB_" + writes[0].name
        return p.do("scalar", lambda e, d=dst_ap, s=src_ap: e.dma_start(out=d, in_=s),
                    reads=reads, writes=writes, inc=sem, n=16)

    def ln_stats(src, src_buf, mv, mvb, sd, sdb, rs, rsb, stt=None, sttb=None):
        if stt is None:
            stt, sttb = stats, "stats"
        segb = []
        for sgi in range(NSEG):
            a, b = sgi * 512, min(D, (sgi + 1) * 512)
            sb_ = gb("%s_%d" % (sttb if isinstance(sttb, str) else sttb.name, sgi))
            segb.append(sb_)
            p.do("vector", lambda e, s=src, a=a, b=b, i=sgi, stt=stt: e.bn_stats(
                out=stt[:, 6 * i:6 * (i + 1)], in_=s[:, a:b]),
                reads=[src_buf], writes=[sb_], inc="v")
        p.do("vector", lambda e, mv=mv, stt=stt: e.bn_aggr(out=mv[:], in_=stt[:, 0:6 * NSEG]),
             reads=segb, writes=[mvb], inc="v")
        p.do("scalar", lambda e, mv=mv, sd=sd: e.activation(out=sd[:], in_=mv[:, 1:2], func=AF.Sqrt, bias=epssb[:], scale=1.0),
             reads=[mvb, gb("epssb")], writes=[sdb], inc="s")
        p.do("vector", lambda e, sd=sd, rs=rs: e.reciprocal(out=rs[:], in_=sd[:]),
             reads=[sdb], writes=[rsb], inc="v")

    def split_gp(src_ap, src_bufs, hi_ap, hi_buf, lo_ap, lo_buf):
        p.do("gpsimd", lambda e: e.tensor_copy(out=hi_ap, in_=src_ap),
             reads=src_bufs, writes=[hi_buf], inc="g")
        p.do("gpsimd", lambda e: e.tensor_tensor(out=lo_ap, in0=src_ap, in1=hi_ap, op=ALU.subtract),
             reads=list(src_bufs) + [hi_buf], writes=[lo_buf], inc="g")

    def split_hilo(src_ap, src_bufs, hi_ap, hi_buf, lo_ap, lo_buf, func=AF.Identity, bias=None, scale=1.0):
        if bias is None:
            p.do("scalar", lambda e: e.activation(out=hi_ap, in_=src_ap, func=func, scale=scale),
                 reads=src_bufs, writes=[hi_buf], inc="s")
        else:
            p.do("scalar", lambda e: e.activation(out=hi_ap, in_=src_ap, func=func, bias=bias, scale=scale),
                 reads=src_bufs, writes=[hi_buf], inc="s")
        p.do("vector", lambda e: e.tensor_tensor(out=lo_ap, in0=src_ap, in1=hi_ap, op=ALU.subtract),
             reads=list(src_bufs) + [hi_buf], writes=[lo_buf], inc="v")

    def prep_consts():
        p.do("gpsimd", lambda e: e.memset(ztmp[:], 1.0), writes=[gb("ztmp")], inc="g")
        p.do("gpsimd", lambda e: e.memset(ones8m[:], 1.0 / K), writes=[gb("ones8m")], inc="g")
        p.do("gpsimd", lambda e: e.memset(ones8s[:], 1.0), writes=[gb("ones8s")], inc="g")
        p.do("gpsimd", lambda e: e.memset(epssb[:], eps), writes=[gb("epssb")], inc="g")
        p.do("gpsimd", lambda e: e.memset(maskt[:], 0.0), writes=[gb("maskt")], inc="g")
        p.do("gpsimd", lambda e: e.affine_select(out=ident[:], in_=ztmp[:], pattern=[[-1, 128]],
                                                 compare_op=ALU.is_equal, fill=0.0, base=0, channel_multiplier=1),
             reads=[gb("ztmp")], writes=[gb("ident")], inc="g")
        p.do("gpsimd", lambda e: e.affine_select(out=maskt[:], in_=maskt[:], pattern=[[-1, 128]],
                                                 compare_op=ALU.is_ge, fill=-1e30, base=0, channel_multiplier=1),
             writes=[gb("maskt")], inc="g")
        p.do("vector", lambda e: e.tensor_copy(out=identb[:], in_=ident[:]),
             reads=[gb("ident")], writes=[gb("identb")], inc="v")
        for name, hsrc, dst in (("b1sb", b1_h, b1sb), ("gsb", gamma_h, gsb), ("bsb", beta_h, bsb)):
            ap_, bk = psc(0, 512)
            dma_in(ytile[0:NC, 0:128], hsrc.rearrange("(a b) -> a b", b=128), writes=[gb("ytile")])
            p.do("tensor", lambda e: e.transpose(out=psC[:, 0:NC], in_=ytile[0:NC, 0:128], identity=ident[0:NC, 0:NC]),
                 reads=[gb("ytile"), gb("ident")], writes=bk, inc="t")
            p.do("vector", lambda e, d=dst: e.tensor_copy(out=d[:, 0:NC], in_=psC[:, 0:NC]),
                 reads=bk, writes=[gb(name)], inc="v")
        dma_in(ytile[0:1, 0:K], b2_h.rearrange("(a b) -> a b", a=1), writes=[gb("ytile")])
        p.do("tensor", lambda e: e.transpose(out=psC[0:K, 0:1], in_=ytile[0:1, 0:K], identity=ident[0:1, 0:1]),
             reads=[gb("ytile"), gb("ident")], writes=pbanks(6, 0, 512), inc="t")
        p.do("vector", lambda e: e.tensor_copy(out=b2sb[0:K, :], in_=psC[0:K, 0:1]),
             reads=pbanks(6, 0, 512), writes=[gb("b2sb")], inc="v")
        for c in range(NC):
            dma_in(ttile[0:128, c * K:(c + 1) * K], W2_h[c * 128:(c + 1) * 128, :], writes=[gb("ttile")], slow=True)
        split_hilo(ttile[:, 0:NC * K], [gb("ttile")], w2hi[:, 0:NC * K], gb("w2hi"), w2lo[:, 0:NC * K], gb("w2lo"))

    VT_BASE = NC * TG
    UT_BASE = VT_BASE + NPAIR * NC * 128

    def vt_slot(c, dc):
        return VT_BASE + (c * NC + dc) * 128

    def ut_base(c):
        return UT_BASE + c * D

    def prep_vu():
        for k in range(K):
            c, odd = divmod(k, 2)
            vb = (ytile, ttile)[k % 2]
            vbb = gb(("ytile", "ttile")[k % 2])
            dma_in(vb[0:R, :], V_h[k, :, :], writes=[vbb])

            def tr_v(e, vb=vb):
                last = None
                for dc in range(NC):
                    last = e.transpose(out=psA[:, dc * R:(dc + 1) * R],
                                       in_=vb[0:R, dc * 128:(dc + 1) * 128], identity=ident[0:R, 0:R])
                return last
            pabk = pbanks(0, 0, NC * R)
            p.do("tensor", tr_v, reads=[vbb, gb("ident")], writes=pabk, inc="t")
            hi_out = bass.AP(bigA_hi, VT_BASE + c * NC * 128 + odd * R, [[BIGW, 128], [128, NC], [1, R]])
            lo_out = bass.AP(bigA_lo, VT_BASE + c * NC * 128 + odd * R, [[BIGW, 128], [128, NC], [1, R]])
            p.do("scalar", lambda e, hi_out=hi_out: e.activation(out=hi_out, in_=psA[:, 0:NC * R], func=AF.Identity),
                 reads=pabk, writes=[gb("bigA_hi")], inc="s")
            p.do("vector", lambda e, hi_out=hi_out, lo_out=lo_out: e.tensor_tensor(
                out=lo_out, in0=psA[:, 0:NC * R], in1=hi_out, op=ALU.subtract),
                reads=pabk + [gb("bigA_hi")], writes=[gb("bigA_lo")], inc="v")
        for k in range(K):
            c, odd = divmod(k, 2)
            ub = (xa0, xa1)[k % 2]
            ubb = gb("xa%d" % (k % 2))
            uin = bass.AP(U_h.tensor, k * D * R, [[R, 128], [128 * R, NC], [1, R]])
            (dma_in if k % 2 == 0 else dma_in_hw)(ub[:, 0:NC * R], uin, writes=[ubb])

            def tr_u(e, ub=ub):
                last = None
                for dc in range(NC):
                    last = e.transpose(out=psA[0:R, dc * 128:(dc + 1) * 128],
                                       in_=ub[:, dc * R:(dc + 1) * R], identity=ident[:])
                return last
            pabk = pbanks(0, 0, NC * 128)
            p.do("tensor", tr_u, reads=[ubb, gb("ident")], writes=pabk, inc="t")
            p.do("scalar", lambda e, odd=odd, c=c: e.activation(
                out=bigA_hi[odd * R:odd * R + R, ut_base(c):ut_base(c) + D], in_=psA[0:R, 0:D], func=AF.Identity),
                reads=pabk, writes=[gb("bigA_hi")], inc="s")
            p.do("vector", lambda e, odd=odd, c=c: e.tensor_tensor(
                out=bigA_lo[odd * R:odd * R + R, ut_base(c):ut_base(c) + D], in0=psA[0:R, 0:D],
                in1=bigA_hi[odd * R:odd * R + R, ut_base(c):ut_base(c) + D], op=ALU.subtract),
                reads=pabk + [gb("bigA_hi")], writes=[gb("bigA_lo")], inc="v")

    def ffn_phase(src_h, dst_h, srcb, dstb, dtag):
        yTh, yTl = gb("bigA_hi"), gb("bigA_lo")
        for g in range(G1):
            t0 = g * TGT
            for i in range(TGT):
                ti = t0 + i
                par = i % 2
                xa = (xa0, xa1)[par]
                xab = gb("xa%d" % par)
                yb = (ytile, ttile)[par]
                ybb = gb(("ytile", "ttile")[par])
                mv_, mvb_ = (mvt, mvt2)[par], (gb("mvt"), gb("mvt2"))[par]
                sd_, sdb_ = (sdt, sdt2)[par], (gb("sdt"), gb("sdt2"))[par]
                rs_, rsb_ = (rstd, rstd2)[par], (gb("rstd"), gb("rstd2"))[par]
                stt_ = (stats, stats2)[par]
                dma_in(xa[:], src_h[ti * 128:(ti + 1) * 128, :], writes=[xab], reads=[srcb])
                ln_stats(xa, xab, mv_, mvb_, sd_, sdb_, rs_, rsb_, stt_, ("stats", "stats2")[par])
                p.do("vector", lambda e, xa=xa, yb=yb, mv_=mv_, rs_=rs_: e.tensor_scalar(
                    out=yb[:], in0=xa[:], scalar1=mv_[:, 0:1], scalar2=rs_[:],
                    op0=ALU.subtract, op1=ALU.mult),
                    reads=[xab, mvb_, rsb_], writes=[ybb], inc="v")

                HNC = NC // 2
                for hf in range(2):
                    ap_, bk = [psa(0, 1024), psa(1024, 2048), psb(0, 1024)][(i * 2 + hf) % 3]

                    def tr_h(e, yb=yb, hf=hf, ap_=ap_, HNC=HNC):
                        last = None
                        for dc in range(HNC):
                            last = e.transpose(out=ap_[:, dc * 128:(dc + 1) * 128],
                                               in_=yb[:, (hf * HNC + dc) * 128:(hf * HNC + dc + 1) * 128],
                                               identity=ident[:])
                        return last
                    p.do("tensor", tr_h, reads=[ybb, gb("ident")], writes=bk, inc="t")
                    hi_out = bass.AP(bigA_hi, hf * HNC * TG + i * 128, [[BIGW, 128], [TG, HNC], [1, 128]])
                    lo_out = bass.AP(bigA_lo, hf * HNC * TG + i * 128, [[BIGW, 128], [TG, HNC], [1, 128]])
                    p.do("scalar", lambda e, hi_out=hi_out, ap_=ap_, HNC=HNC: e.activation(
                        out=hi_out, in_=ap_[:, 0:HNC * 128], func=AF.Identity),
                        reads=bk, writes=[yTh], inc="s")
                    p.do("vector", lambda e, hi_out=hi_out, lo_out=lo_out, ap_=ap_, HNC=HNC: e.tensor_tensor(
                        out=lo_out, in0=ap_[:, 0:HNC * 128], in1=hi_out, op=ALU.subtract),
                        reads=bk + [yTh], writes=[yTl], inc="v")

            if cut < 3:
                continue
            hb = [psa(0, TG), psa(1024, 1024 + TG)]
            hsb = [(ht_hi, ht_lo, gb("ht_hi"), gb("ht_lo")), (ht2_hi, ht2_lo, gb("ht2_hi"), gb("ht2_lo"))]
            lg_ap, lg_bk = psb(0, TG)
            pend_router = []

            def emit_router():
                if pend_router:
                    fn, rd = pend_router.pop()
                    p.do("tensor", fn, reads=rd, writes=lg_bk, inc="t")

            chunks = [(do_, di) for do_ in range(NC) for di in range(NC)]
            if cfg.get("maxmm") is not None:
                chunks = chunks[:cfg.get("maxmm")]

            def w1_load(j):
                do_, di = chunks[j]
                ws = j % 4
                dma_in(w1f[:, ws * 128:(ws + 1) * 128],
                       W1_h[di * 128:(di + 1) * 128, do_ * 128:(do_ + 1) * 128], writes=[gb("w1f%d" % ws)])
                split_gp(w1f[:, ws * 128:(ws + 1) * 128], [gb("w1f%d" % ws)],
                         w1hi[:, ws * 128:(ws + 1) * 128], gb("w1hi%d" % ws),
                         w1lo[:, ws * 128:(ws + 1) * 128], gb("w1lo%d" % ws))

            for j in range(min(4, len(chunks))):
                w1_load(j)
            for j, (do_, di) in enumerate(chunks):
                ws = j % 4
                pa, pab = hb[do_ % 2]

                def mm_h(e, di=di, ws=ws, pa=pa):
                    last = None
                    for pi, (wop, yop) in enumerate(((w1hi, bigA_hi), (w1hi, bigA_lo), (w1lo, bigA_hi))):
                        for nt in range(cdiv(TG, 512)):
                            n0, n1 = nt * 512, min(TG, (nt + 1) * 512)
                            last = e.matmul(out=pa[:, n0:n1], lhsT=wop[:, ws * 128:(ws + 1) * 128],
                                            rhs=yop[:, di * TG + n0:di * TG + n1],
                                            start=(di == 0 and pi == 0), stop=(di == NC - 1 and pi == 2))
                    return last
                p.do("tensor", mm_h, reads=[gb("w1hi%d" % ws), gb("w1lo%d" % ws), yTh, yTl], writes=pab, inc="t")
                if j + 4 < len(chunks):
                    w1_load(j + 4)
                if di == NC - 1:
                    hh, hl, hhb, hlb = hsb[do_ % 2]
                    p.do("vector", lambda e, pa=pa, do_=do_: e.tensor_scalar(
                        out=ttile[:, 0:TG], in0=pa[:, 0:TG], scalar1=b1sb[:, do_:do_ + 1], scalar2=0.0,
                        op0=ALU.add, op1=ALU.max),
                        reads=pab + [gb("b1sb")], writes=[gb("ttile")], inc="v")
                    p.do("scalar", lambda e, hh=hh: e.activation(
                        out=hh[:, 0:TG], in_=ttile[:, 0:TG], func=AF.Identity),
                        reads=[gb("ttile")], writes=[hhb], inc="s")
                    p.do("vector", lambda e, hh=hh, hl=hl: e.tensor_tensor(
                        out=hl[:, 0:TG], in0=ttile[:, 0:TG], in1=hh[:, 0:TG], op=ALU.subtract),
                        reads=[gb("ttile"), hhb], writes=[hlb], inc="v")

                    def mm_router(e, do_=do_, hh=hh, hl=hl):
                        last = None
                        for pi, (wop, hop) in enumerate(((w2hi, hh), (w2hi, hl), (w2lo, hh))):
                            for nt in range(cdiv(TG, 512)):
                                n0, n1 = nt * 512, min(TG, (nt + 1) * 512)
                                last = e.matmul(out=psB[0:K, n0:n1], lhsT=wop[:, do_ * K:(do_ + 1) * K],
                                                rhs=hop[:, n0:n1],
                                                start=(do_ == 0 and pi == 0), stop=(do_ == NC - 1 and pi == 2))
                        return last
                    emit_router()
                    if not cfg.get("no_router"):
                        pend_router.append((mm_router, [hhb, hlb, gb("w2hi"), gb("w2lo")]))
            emit_router()

            if cut < 4:
                continue
            p.do("scalar", lambda e: e.activation(out=logt[0:K, 0:TG], in_=psB[0:K, 0:TG], func=AF.Identity,
                                                  bias=b2sb[0:K, :], scale=1.0),
                 reads=lg_bk + [gb("b2sb")], writes=[gb("albc")], inc="s")
            def bcast_via_hbm(src_sb, src_buf, hbt, hb_ap, nreps, cols):
                dma_in(hb_ap[0:1, 0:cols], src_sb, writes=[gb(hbt)], reads=[src_buf])
                return bass.AP(hb_ap.tensor, hb_ap.offset, [[0, nreps], [1, cols]])

            def mm_mean(e):
                last = None
                for nt in range(cdiv(TG, 512)):
                    n0, n1 = nt * 512, min(TG, (nt + 1) * 512)
                    last = e.matmul(out=psB[0:1, n0:n1], lhsT=ones8m[0:K, :], rhs=logt[0:K, n0:n1],
                                    start=True, stop=True)
                return last
            p.do("tensor", mm_mean, reads=[gb("albc"), gb("ones8m")], writes=lg_bk, inc="t")
            p.do("vector", lambda e: e.tensor_copy(out=lmax[0:1, 0:TG], in_=psB[0:1, 0:TG]),
                 reads=lg_bk, writes=[gb("albc")], inc="v")
            rep = bcast_via_hbm(lmax[0:1, 0:TG], gb("albc"), "lmax_h", lmax_h, K, TG)
            dma_in(lmaxB[0:K, 0:TG], rep, writes=[gb("albc")], reads=[gb("lmax_h")])
            p.do("vector", lambda e: e.tensor_tensor(out=logt[0:K, 0:TG], in0=logt[0:K, 0:TG],
                                                     in1=lmaxB[0:K, 0:TG], op=ALU.subtract),
                 reads=[gb("albc")], writes=[gb("albc")], inc="v")
            p.do("scalar", lambda e: e.activation(out=logt[0:K, 0:TG], in_=logt[0:K, 0:TG], func=AF.Exp),
                 writes=[gb("albc")], inc="s")
            def mm_sum(e):
                last = None
                for nt in range(cdiv(TG, 512)):
                    n0, n1 = nt * 512, min(TG, (nt + 1) * 512)
                    last = e.matmul(out=psB[0:1, n0:n1], lhsT=ones8s[0:K, :], rhs=logt[0:K, n0:n1],
                                    start=True, stop=True)
                return last
            p.do("tensor", mm_sum, reads=[gb("albc"), gb("ones8s")], writes=lg_bk, inc="t")
            p.do("vector", lambda e: e.reciprocal(out=lmax[0:1, 0:TG], in_=psB[0:1, 0:TG]),
                 reads=lg_bk, writes=[gb("albc")], inc="v")
            rep = bcast_via_hbm(lmax[0:1, 0:TG], gb("albc"), "lmax_h", lmax_h, K, TG)
            dma_in(lmaxB[0:K, 0:TG], rep, writes=[gb("albc")], reads=[gb("lmax_h")])
            p.do("vector", lambda e: e.tensor_tensor(out=alphaT[0:K, 0:TG], in0=logt[0:K, 0:TG],
                                                     in1=lmaxB[0:K, 0:TG], op=ALU.mult),
                 reads=[gb("albc"), gb("albc")], writes=[gb("albc")], inc="v")
            dma_in(alpha_h[0:K, 0:TG], alphaT[0:K, 0:TG], writes=[gb("alpha_h")], reads=[gb("albc")])

            if cut < 5:
                continue
            for c in range(NPAIR):
                pa, pab = hb[c % 2]
                rep0 = bass.AP(alpha_hb, (2 * c) * 1024, [[0, R], [1, TG]])
                rep1 = bass.AP(alpha_hb, (2 * c + 1) * 1024, [[0, 128 - R], [1, TG]])
                dma_in(albc[0:R, 0:TG], rep0, writes=[gb("albc")], reads=[gb("alpha_h")])
                dma_in(albc[R:128, 0:TG], rep1, writes=[gb("albc")], reads=[gb("alpha_h")])

                def mm_vx(e, c=c, pa=pa):
                    last = None
                    for pi, (wsel, ysel) in enumerate(((bigA_hi, bigA_hi), (bigA_hi, bigA_lo), (bigA_lo, bigA_hi))):
                        for di in range(NC):
                            for nt in range(cdiv(TG, 512)):
                                n0, n1 = nt * 512, min(TG, (nt + 1) * 512)
                                last = e.matmul(out=pa[:, n0:n1],
                                                lhsT=wsel[:, vt_slot(c, di):vt_slot(c, di) + 128],
                                                rhs=ysel[:, di * TG + n0:di * TG + n1],
                                                start=(pi == 0 and di == 0), stop=(pi == 2 and di == NC - 1))
                    return last
                p.do("tensor", mm_vx, reads=[yTh, yTl], writes=pab, inc="t")
                p.do("vector", lambda e, pa=pa: e.tensor_tensor(out=ttile[:, 0:TG], in0=pa[:, 0:TG],
                                                                in1=albc[:, 0:TG], op=ALU.mult),
                     reads=pab + [gb("albc")], writes=[gb("ttile")], inc="v")
                split_hilo(ttile[:, 0:TG], [gb("ttile")], sv_hi[:, c * TG:(c + 1) * TG], gb("sv_hi"),
                           sv_lo[:, c * TG:(c + 1) * TG], gb("sv_lo"))

            if cut < 6:
                continue
            DH = min(D, 1024)
            NDH = cdiv(D, DH)
            rot = [psa(0, DH), psa(1024, 1024 + DH), psb(0, DH)]
            ri = 0
            for i in range(TGT):
                ti = t0 + i
                xr = (ytile, ttile)[i % 2]
                xrb = gb(("ytile", "ttile")[i % 2])
                dma_in(xr[:], src_h[ti * 128:(ti + 1) * 128, :], writes=[xrb], reads=[srcb])
                ob = (xa0, xa1)[i % 2]
                obb = gb("xa%d" % (i % 2))
                for dh in range(NDH):
                    pa, pab = rot[ri % 3]
                    ri += 1

                    DW = min(DH, D - dh * DH)

                    def mm_uvx(e, i=i, dh=dh, pa=pa, DW=DW):
                        last = None
                        for pi, (ssel, usel) in enumerate(((sv_hi, bigA_hi), (sv_hi, bigA_lo), (sv_lo, bigA_hi))):
                            for c in range(NPAIR):
                                for nt in range(cdiv(DW, 512)):
                                    n0, n1 = nt * 512, min(DW, (nt + 1) * 512)
                                    last = e.matmul(
                                        out=pa[:, n0:n1],
                                        lhsT=ssel[:, c * TG + i * 128:c * TG + (i + 1) * 128],
                                        rhs=usel[:, ut_base(c) + dh * DH + n0:ut_base(c) + dh * DH + n1],
                                        start=(pi == 0 and c == 0), stop=(pi == 2 and c == NPAIR - 1))
                        return last
                    p.do("tensor", mm_uvx, reads=[gb("sv_hi"), gb("sv_lo"), yTh, yTl], writes=pab, inc="t")
                    p.do("vector", lambda e, pa=pa, dh=dh, xr=xr, ob=ob, DW=DW: e.scalar_tensor_tensor(
                        out=ob[:, dh * DH:dh * DH + DW], in0=pa[:, 0:DW], scalar=0.5,
                        in1=xr[:, dh * DH:dh * DH + DW], op0=ALU.mult, op1=ALU.add),
                        reads=pab + [xrb], writes=[obb], inc="v")
                dma_out(dst_h[ti * 128:(ti + 1) * 128, :], ob[:], reads=[obb], writes=[dstb])
        p.flush_block()


    def proj_phase():
        zTh, zTl = gb("bigA_hi"), gb("bigA_lo")
        THW = min(1024, S)
        NTH = cdiv(S, THW)
        TPH = THW // 128

        def emit_a_ln(ti):
            xa = (xa0, xa1)[ti % 2]
            xab = gb("xa%d" % (ti % 2))
            dma_in(xa[:], x1_h[ti * 128:(ti + 1) * 128, :], writes=[xab], reads=[gb("x1_h")])
            ln_stats(xa, xab, mvt, gb("mvt"), sdt, gb("sdt"), rstd, gb("rstd"))
            p.do("vector", lambda e, xa=xa: e.tensor_scalar(
                out=ytile[:], in0=xa[:], scalar1=mvt[:, 0:1], scalar2=rstd[:],
                op0=ALU.subtract, op1=ALU.mult),
                reads=[xab, gb("mvt"), gb("rstd")], writes=[gb("ytile")], inc="v")
            ln_stats(ytile, gb("ytile"), mvt2, gb("mvt2"), sdt2, gb("sdt2"), rstd2, gb("rstd2"),
                     stats2, "stats2")
            p.do("vector", lambda e: e.tensor_scalar(
                out=ttile[:], in0=ytile[:], scalar1=mvt2[:, 0:1], scalar2=rstd2[:],
                op0=ALU.subtract, op1=ALU.mult),
                reads=[gb("ytile"), gb("mvt2"), gb("rstd2")], writes=[gb("ttile")], inc="v")
            p.do("vector", lambda e, xa=xa: e.tensor_tensor(out=xa[:], in0=xa[:], in1=ytile[:], op=ALU.add),
                 reads=[gb("ytile")], writes=[xab], inc="v")
            dma_out(xp_h[ti * 128:(ti + 1) * 128, :], xa[:], reads=[xab], writes=[gb("xp_h")])

        def emit_a_tr(ti, use_psc):
            if not use_psc:
                def tr_all(e):
                    last = None
                    for dc in range(NC):
                        last = e.transpose(out=psA[:, dc * 128:(dc + 1) * 128],
                                           in_=ttile[:, dc * 128:(dc + 1) * 128], identity=ident[:])
                    return last
                pabk = pbanks(0, 0, NC * 128)
                p.do("tensor", tr_all, reads=[gb("ttile"), gb("ident")], writes=pabk, inc="t")
                for dc in range(NC):
                    p.do("scalar", lambda e, dc=dc: e.activation(
                        out=albc[:, dc * 128:(dc + 1) * 128], in_=psA[:, dc * 128:(dc + 1) * 128],
                        func=AF.Identity, bias=bsb[:, dc:dc + 1], scale=gsb[:, dc:dc + 1]),
                        reads=pabk + [gb("gsb"), gb("bsb")], writes=[gb("albc")], inc="s")
            else:
                BCH = max(1, NC // 4)
                for b0 in range(0, NC, BCH):
                    nb = min(BCH, NC - b0)
                    def tr_b(e, b0=b0, nb=nb):
                        last = None
                        for j in range(nb):
                            last = e.transpose(out=psC[:, j * 128:(j + 1) * 128],
                                               in_=ttile[:, (b0 + j) * 128:(b0 + j + 1) * 128],
                                               identity=ident[:])
                        return last
                    p.do("tensor", tr_b, reads=[gb("ttile"), gb("ident")], writes=[PB[6]], inc="t")
                    for j in range(nb):
                        p.do("scalar", lambda e, b0=b0, j=j: e.activation(
                            out=albc[:, (b0 + j) * 128:(b0 + j + 1) * 128], in_=psC[:, j * 128:(j + 1) * 128],
                            func=AF.Identity, bias=bsb[:, b0 + j:b0 + j + 1], scale=gsb[:, b0 + j:b0 + j + 1]),
                            reads=[PB[6], gb("gsb"), gb("bsb")], writes=[gb("albc")], inc="s")
            hi_out = bass.AP(bigA_hi, ti * 128, [[BIGW, 128], [S, NC], [1, 128]])
            lo_out = bass.AP(bigA_lo, ti * 128, [[BIGW, 128], [S, NC], [1, 128]])
            p.do("scalar", lambda e, hi_out=hi_out: e.activation(
                out=hi_out, in_=albc[:, 0:NC * 128], func=AF.Identity),
                reads=[gb("albc")], writes=[zTh], inc="s")
            p.do("vector", lambda e, hi_out=hi_out, lo_out=lo_out: e.tensor_tensor(
                out=lo_out, in0=albc[:, 0:NC * 128], in1=hi_out, op=ALU.subtract),
                reads=[gb("albc"), zTh], writes=[zTl], inc="v")

        for ti in range(TPH):
            emit_a_ln(ti)
            emit_a_tr(ti, use_psc=False)

        pend = []

        def pop_pend(n=1):
            for _ in range(n):
                if pend:
                    pend.pop(0)()
        for ti in range(TPH, NT):
            pend.append(lambda ti=ti: emit_a_ln(ti))
            pend.append(lambda ti=ti: emit_a_tr(ti, use_psc=True))

        chunks = [(th, do_, di) for th in range(NTH) for do_ in range(3 * NC) for di in range(NC)]

        def qkv_load(j):
            th, do_, di = chunks[j]
            ws = j % 4
            dma_in(w1f[:, ws * 128:(ws + 1) * 128],
                   qkv_h[di * 128:(di + 1) * 128, do_ * 128:(do_ + 1) * 128], writes=[gb("w1f%d" % ws)])
            split_hilo(w1f[:, ws * 128:(ws + 1) * 128], [gb("w1f%d" % ws)],
                       w1hi[:, ws * 128:(ws + 1) * 128], gb("w1hi%d" % ws),
                       w1lo[:, ws * 128:(ws + 1) * 128], gb("w1lo%d" % ws))

        for j in range(min(4, len(chunks))):
            qkv_load(j)
        vpend = []
        rot = [psa(0, THW), psa(1024, min(2048, 1024 + THW)), psb(0, THW)]
        ri = 0
        for j, (th, do_, di) in enumerate(chunks):
            ws = j % 4
            THL = min(THW, S - th * THW)
            if di == 0:
                pa, pab = rot[ri % 3]
                ri += 1

            def mm_qkv(e, di=di, ws=ws, pa=pa, th=th, THL=THL):
                last = None
                for pi, (wop, zop) in enumerate(((w1hi, bigA_hi), (w1hi, bigA_lo), (w1lo, bigA_hi))):
                    for nt in range(cdiv(THL, 512)):
                        n0 = th * THW + nt * 512
                        n1 = th * THW + min(THL, (nt + 1) * 512)
                        last = e.matmul(out=pa[:, n0 - th * THW:n1 - th * THW],
                                        lhsT=wop[:, ws * 128:(ws + 1) * 128],
                                        rhs=zop[:, di * S + n0:di * S + n1],
                                        start=(di == 0 and pi == 0), stop=(di == NC - 1 and pi == 2))
                return last
            p.do("tensor", mm_qkv, reads=[gb("w1hi%d" % ws), gb("w1lo%d" % ws), zTh, zTl], writes=pab, inc="t")
            if j + 4 < len(chunks):
                qkv_load(j + 4)
            if di != NC - 1:
                continue
            if th == 0:
                pop_pend(1)
            if do_ == 0 and th > 0:
                while vpend:
                    vpend.pop(0)()
            while len(vpend) >= 2:
                vpend.pop(0)()
            hh, hl, hhb, hlb = ((ht_hi, ht_lo, gb("ht_hi"), gb("ht_lo")),
                                (ht2_hi, ht2_lo, gb("ht2_hi"), gb("ht2_lo")))[do_ % 2]
            if do_ < 2 * NC:
                dst_hi, dst_lo = (q_hi_h, q_lo_h) if do_ < NC else (k_hi_h, k_lo_h)
                row = (do_ % NC) * 128
                split_hilo(pa[:, 0:THL], pab, hh[:, 0:THL], hhb, hl[:, 0:THL], hlb)
                dma_out(dst_hi[row:row + 128, th * THW:th * THW + THL], hh[:, 0:THL], reads=[hhb],
                        writes=[gb("q_h") if do_ < NC else gb("k_h")])
                dma_out(dst_lo[row:row + 128, th * THW:th * THW + THL], hl[:, 0:THL], reads=[hlb],
                        writes=[gb("q_h") if do_ < NC else gb("k_h")])
            else:
                dv = do_ - 2 * NC
                split_hilo(pa[:, 0:THL], pab, hh[:, 0:THL], hhb, hl[:, 0:THL], hlb)

                def pend_v(dv=dv, hh=hh, hl=hl, hhb=hhb, hlb=hlb, th=th, THL=THL):
                    TPH_ = THL // 128
                    for hb_, hbb, voh in ((hh, hhb, v_hi_h), (hl, hlb, v_lo_h)):
                        for b0 in range(0, TPH_, 8):
                            nb = min(8, TPH_ - b0)
                            def tr8(e, hb_=hb_, b0=b0, nb=nb):
                                last = None
                                for t_ in range(nb):
                                    last = e.transpose(out=psT[:, t_ * 128:(t_ + 1) * 128],
                                                       in_=hb_[:, (b0 + t_) * 128:(b0 + t_ + 1) * 128],
                                                       identity=identb[:])
                                return last
                            p.do("tensor", tr8, reads=[hbb, gb("identb")], writes=[PB[7]], inc="t")
                            p.do("vector", lambda e, hb_=hb_, b0=b0, nb=nb: e.tensor_copy(
                                out=hb_[:, b0 * 128:(b0 + nb) * 128], in_=psT[:, 0:nb * 128]),
                                reads=[PB[7]], writes=[hbb], inc="v")
                        vo = bass.AP(voh.tensor, (th * THW) * D + dv * 128,
                                     [[D, 128], [128 * D, TPH_], [1, 128]])
                        dma_out(vo, hb_[:, 0:THL], reads=[hbb], writes=[gb("v_h")])
                vpend.append(pend_v)
        while vpend:
            vpend.pop(0)()
        pop_pend(99)
        p.flush_block()

    def scores_phase():
        p.do("gpsimd", lambda e: e.memset(zbf[:], 0.0), writes=[gb("zbf")], inc="g")

        TPG = QG // 128
        for i in range(NT):
            gmax = (i // TPG) * TPG + TPG - 1
            for kc in range(i + 1, min(gmax, NT - 1) + 1):
                for dsth in (pt_hi_h, pt_lo_h):
                    dma_out(dsth[kc * 128:(kc + 1) * 128, i * 128:(i + 1) * 128], zbf[:],
                            reads=[gb("zbf")], writes=[gb("pt_h")])
        KT0 = min(512, S)
        Kearly, Klate = gb("K_early"), gb("K_late")
        for dc in range(NC):
            dma_in(bigA_hi[:, dc * S:dc * S + KT0], k_hi_h[dc * 128:(dc + 1) * 128, 0:KT0],
                   writes=[Kearly], reads=[gb("k_h")])
            dma_in(bigA_lo[:, dc * S:dc * S + KT0], k_lo_h[dc * 128:(dc + 1) * 128, 0:KT0],
                   writes=[Kearly], reads=[gb("k_h")])
        if S > KT0:
            for dc in range(NC):
                dma_in(bigA_hi[:, dc * S + KT0:(dc + 1) * S], k_hi_h[dc * 128:(dc + 1) * 128, KT0:],
                       writes=[Klate], reads=[gb("k_h")])
                dma_in(bigA_lo[:, dc * S + KT0:(dc + 1) * S], k_lo_h[dc * 128:(dc + 1) * 128, KT0:],
                       writes=[Klate], reads=[gb("k_h")])
        rot = [psa(0, 1024), psa(1024, 2048), psb(0, 1024)]
        ri = 0
        pend = []

        def emit_pend():
            while pend:
                pend.pop(0)()

        for i in range(NT):
            keys = (i + 1) * 128
            qh = gb("qt_hi%d" % (i % 2))
            ql = gb("qt_lo%d" % (i % 2))
            QW = NC * 128
            qhs = sv_hi[:, (i % 2) * QW:(i % 2) * QW + QW]
            qls = sv_lo[:, (i % 2) * QW:(i % 2) * QW + QW]
            for half, (hbuf, dsth) in enumerate(((qhs, q_hi_h), (qls, q_lo_h))):
                for quarter in range(4):
                    c0 = quarter * (NC // 4) if NC >= 4 else 0
                    c1 = (quarter + 1) * (NC // 4) if NC >= 4 else NC
                    if NC < 4 and quarter > 0:
                        continue
                    ap_in = bass.AP(dsth.tensor, (c0 * 128) * S + i * 128,
                                    [[S, 128], [128 * S, c1 - c0], [1, 128]])
                    dma_in_hw(hbuf[:, c0 * 128:c1 * 128], ap_in, writes=[qh if half == 0 else ql], reads=[gb("q_h")])
            nseg = cdiv(keys, 1024)
            segbufs = []
            for sg in range(nseg):
                s0, s1 = sg * 1024, min(keys, (sg + 1) * 1024)
                pa, pab = rot[ri % 3]
                ri += 1
                segbufs.append((s0, s1, pa, pab))

                def mm_sc1(e, i=i, s0=s0, s1=s1, pa=pa, qhs=qhs):
                    last = None
                    for dc in range(NC):
                        for nt in range(cdiv(s1 - s0, 512)):
                            n0 = s0 + nt * 512
                            n1 = min(s1, n0 + 512)
                            last = e.matmul(out=pa[:, n0 - s0:n1 - s0],
                                            lhsT=qhs[:, dc * 128:(dc + 1) * 128],
                                            rhs=bigA_hi[:, dc * S + n0:dc * S + n1],
                                            start=(dc == 0), stop=False)
                    return last

                def mm_sc23(e, i=i, s0=s0, s1=s1, pa=pa, qhs=qhs, qls=qls):
                    last = None
                    for pi, (qop, kop) in enumerate(((qhs, bigA_lo), (qls, bigA_hi))):
                        for dc in range(NC):
                            for nt in range(cdiv(s1 - s0, 512)):
                                n0 = s0 + nt * 512
                                n1 = min(s1, n0 + 512)
                                last = e.matmul(out=pa[:, n0 - s0:n1 - s0],
                                                lhsT=qop[:, dc * 128:(dc + 1) * 128],
                                                rhs=kop[:, dc * S + n0:dc * S + n1],
                                                start=False, stop=(pi == 1 and dc == NC - 1))
                    return last
                kreads = [Kearly] + ([Klate] if s1 > KT0 else [])
                p.do("tensor", mm_sc1, reads=[qh] + kreads, writes=pab, inc="t")
                p.do("tensor", mm_sc23, reads=[qh, ql] + kreads, writes=pab, inc="t")

            hh, hhb = (ht_hi, gb("ht_hi")) if i % 2 == 0 else (ht2_hi, gb("ht2_hi"))
            hl, hlb = (ht_lo, gb("ht_lo")) if i % 2 == 0 else (ht2_lo, gb("ht2_lo"))
            seeded = False
            for si, (s0, s1, pa, pab) in enumerate(segbufs):
                dlen = 128 if s1 == keys else 0
                if s1 - dlen > s0:
                    seed = nmax[:] if seeded else -1e30
                    p.do("vector", lambda e, pa=pa, s0=s0, s1=s1, dlen=dlen, seed=seed: e.tensor_scalar(
                        out=ytile[:, s0:s1 - dlen], in0=pa[:, 0:s1 - dlen - s0], scalar1=smax_scale,
                        scalar2=seed, op0=ALU.mult, op1=ALU.max, accum_out=nmax[:]),
                        reads=pab + [gb("nmax")], writes=[gb("ytile"), gb("nmax")], inc="v")
                    seeded = True
                if dlen:
                    p.do("vector", lambda e, pa=pa, s0=s0, s1=s1: e.scalar_tensor_tensor(
                        out=ytile[:, s1 - 128:s1], in0=pa[:, s1 - 128 - s0:s1 - s0], scalar=smax_scale,
                        in1=maskt[:], op0=ALU.mult, op1=ALU.add),
                        reads=pab + [gb("maskt")], writes=[gb("ytile")], inc="v")
                    seed = nmax[:] if seeded else -1e30
                    p.do("vector", lambda e, s1=s1, seed=seed: e.tensor_scalar(
                        out=ttile[:, 0:128], in0=ytile[:, s1 - 128:s1], scalar1=1.0,
                        scalar2=seed, op0=ALU.mult, op1=ALU.max, accum_out=nmax[:]),
                        reads=[gb("ytile"), gb("nmax")], writes=[gb("ttile"), gb("nmax")], inc="v")
                    seeded = True
            p.do("vector", lambda e: e.tensor_scalar(out=nmax[:], in0=nmax[:], scalar1=-1.0, scalar2=None,
                                                     op0=ALU.mult),
                 writes=[gb("nmax")], inc="v")
            p.do("scalar", lambda e, keys=keys: e.activation(out=albc[:, 0:keys], in_=ytile[:, 0:keys],
                                                             func=AF.Exp, bias=nmax[:], scale=1.0,
                                                             accum_out=lsum[:]),
                 reads=[gb("ytile"), gb("nmax")], writes=[gb("albc"), gb("lsum")], inc="s")
            p.do("vector", lambda e: e.reciprocal(out=rinv[:], in_=lsum[:]),
                 reads=[gb("lsum")], writes=[gb("rinv")], inc="v")
            p.do("vector", lambda e, keys=keys, hh=hh: e.tensor_scalar(
                out=hh[:, 0:keys], in0=albc[:, 0:keys], scalar1=rinv[:], scalar2=None, op0=ALU.mult),
                reads=[gb("albc"), gb("rinv")], writes=[hhb], inc="v")
            p.do("vector", lambda e, keys=keys, hh=hh, hl=hl: e.scalar_tensor_tensor(
                out=hl[:, 0:keys], in0=albc[:, 0:keys], scalar=rinv[:], in1=hh[:, 0:keys],
                op0=ALU.mult, op1=ALU.subtract),
                reads=[gb("albc"), gb("rinv"), hhb], writes=[hlb], inc="v")

            def tr_pt(i=i, hh=hh, hl=hl, hhb=hhb, hlb=hlb):
                for b0 in range(0, i + 1, 8):
                    nb = min(8, i + 1 - b0)
                    for hb_, hbb, dsth in ((hh, hhb, pt_hi_h), (hl, hlb, pt_lo_h)):
                        def tr8(e, hb_=hb_, b0=b0, nb=nb):
                            last = None
                            for j in range(nb):
                                last = e.transpose(out=psT[:, j * 128:(j + 1) * 128],
                                                   in_=hb_[:, (b0 + j) * 128:(b0 + j + 1) * 128],
                                                   identity=identb[:])
                            return last
                        p.do("tensor", tr8, reads=[hbb, gb("identb")], writes=[PB[7]], inc="t")
                        p.do("vector", lambda e, hb_=hb_, b0=b0, nb=nb: e.tensor_copy(
                            out=hb_[:, b0 * 128:(b0 + nb) * 128], in_=psT[:, 0:nb * 128]),
                            reads=[PB[7]], writes=[hbb], inc="v")
                        dst = bass.AP(dsth.tensor, b0 * 128 * S + i * 128,
                                      [[S, 128], [128 * S, nb], [1, 128]])
                        dma_out(dst, hb_[:, b0 * 128:(b0 + nb) * 128], reads=[hbb], writes=[gb("pt_h")])
            pend.append(tr_pt)
            if len(pend) > 1:
                pend.pop(0)()
        emit_pend()
        p.flush_block()

    def av_phase():
        vhi_bufs = [gb("Vh_kc%d" % kc) for kc in range(NT)]
        vlo_bufs = [gb("Vl_kc%d" % kc) for kc in range(NT)]
        SLOTS = min(16, (2 * S) // QG + (2 * D) // QG)
        def slot_ap(hi, s_):
            base = (sv_hi, sv_lo)[0 if hi else 1]
            n_sv = (2 * S) // QG
            if s_ < n_sv:
                return (sv_hi if hi else sv_lo)[:, s_ * QG:(s_ + 1) * QG]
            s2 = s_ - n_sv
            n_ht = D // QG
            if s2 < n_ht:
                return (ht_hi if hi else ht_lo)[:, s2 * QG:(s2 + 1) * QG]
            s3 = s2 - n_ht
            return (ht2_hi if hi else ht2_lo)[:, s3 * QG:(s3 + 1) * QG]
        slot_buf = [gb("ptsl%d" % s_) for s_ in range(SLOTS)]
        slot_seq = [0]
        slot_of = {}

        def load_pt(g):
            nkc = (g * QG + QG) // 128
            for kc in range(nkc):
                s_ = slot_seq[0] % SLOTS
                slot_seq[0] += 1
                slot_of[(g, kc)] = s_
                dma_in_hw(slot_ap(True, s_),
                          pt_hi_h[kc * 128:(kc + 1) * 128, g * QG:(g + 1) * QG],
                          writes=[slot_buf[s_]], reads=[gb("pt_h")])
                dma_in_hw(slot_ap(False, s_),
                          pt_lo_h[kc * 128:(kc + 1) * 128, g * QG:(g + 1) * QG],
                          writes=[slot_buf[s_]], reads=[gb("pt_h")])

        load_pt(0)
        for kc in range(NT):
            dma_in(bigA_hi[:, kc * D:(kc + 1) * D], v_hi_h[kc * 128:(kc + 1) * 128, :],
                   writes=[vhi_bufs[kc]], reads=[gb("v_h")])
        for kc in range(NT):
            dma_in(bigA_lo[:, kc * D:(kc + 1) * D], v_lo_h[kc * 128:(kc + 1) * 128, :],
                   writes=[vlo_bufs[kc]], reads=[gb("v_h")])
        for g in range(NQG):
            nkc = (g * QG + QG) // 128
            myslots = [slot_buf[slot_of[(g, kc)]] for kc in range(nkc)]
            for dv in range(NC):
                pa = psA[:, (dv % 4) * 512:(dv % 4) * 512 + QG]
                pab = [PB[dv % 4]] if QG <= 512 else None

                def mm_av12(e, g=g, dv=dv, nkc=nkc, pa=pa):
                    last = None
                    for pi in range(2):
                        for kc in range(nkc):
                            s_ = slot_of[(g, kc)]
                            last = e.matmul(out=pa, lhsT=bigA_hi[:, kc * D + dv * 128:kc * D + (dv + 1) * 128],
                                            rhs=slot_ap(pi == 0, s_),
                                            start=(pi == 0 and kc == 0), stop=False)
                    return last

                def mm_av3(e, g=g, dv=dv, nkc=nkc, pa=pa):
                    last = None
                    for kc in range(nkc):
                        s_ = slot_of[(g, kc)]
                        last = e.matmul(out=pa, lhsT=bigA_lo[:, kc * D + dv * 128:kc * D + (dv + 1) * 128],
                                        rhs=slot_ap(True, s_),
                                        start=False, stop=(kc == nkc - 1))
                    return last
                p.do("tensor", mm_av12, reads=vhi_bufs[:nkc] + myslots, writes=pab, inc="t")
                p.do("tensor", mm_av3, reads=vlo_bufs[:nkc] + myslots, writes=pab, inc="t")
                nsa = max(1, S // QG)
                asl = albc[:, (dv % nsa) * QG:(dv % nsa) * QG + QG]
                p.do("scalar", lambda e, pa=pa, asl=asl: e.activation(out=asl, in_=pa, func=AF.Identity),
                     reads=pab, writes=[gb("albc")], inc="s")
                dma_out(aot_f_h[dv * 128:(dv + 1) * 128, g * QG:(g + 1) * QG], asl,
                        reads=[gb("albc")], writes=[gb("aot_h")])
            if g + 1 < NQG:
                load_pt(g + 1)
        p.flush_block()

    def oproj_phase():
        ochunk = [gb("O_dv%d" % dv) for dv in range(NC)]

        def load_tile(i):
            af = gb("aot_f%d" % (i % 2))
            QW = NC * 128
            fb = (ytile, albc)[i % 2]
            fbb = gb(("ytile", "albc")[i % 2])
            for quarter in range(4):
                c0 = quarter * (NC // 4) if NC >= 4 else 0
                c1 = (quarter + 1) * (NC // 4) if NC >= 4 else NC
                if NC < 4 and quarter > 0:
                    continue
                ap_in = bass.AP(aot_f_h.tensor, (c0 * 128) * S + i * 128,
                                [[S, 128], [128 * S, c1 - c0], [1, 128]])
                dma_in(fb[:, c0 * 128:c1 * 128], ap_in, writes=[fbb], reads=[gb("aot_h")])
            xa = (xa0, xa1)[i % 2]
            xab = gb("xa%d" % (i % 2))
            dma_in(xa[:], xp_h[i * 128:(i + 1) * 128, :], writes=[xab], reads=[gb("xp_h")])
            return af, fb, fbb, i, xa, xab

        def split_tile(af, fb, fbb, i):
            ah = gb("aot_hi%d" % (i % 2))
            al = gb("aot_lo%d" % (i % 2))
            QW = NC * 128
            ahs = sv_hi[:, (i % 2) * QW:(i % 2) * QW + QW]
            als = sv_lo[:, (i % 2) * QW:(i % 2) * QW + QW]
            split_hilo(fb[:, 0:QW], [fbb], ahs, ah, als, al)
            return ah, al, ahs, als

        t_loaded = load_tile(0)
        for dv in range(NC):
            fb = ttile
            fbb = gb("ttile")
            dma_in(fb[:, 0:D], oproj_h[dv * 128:(dv + 1) * 128, :], writes=[fbb])
            split_hilo(fb[:, 0:D], [fbb], bigA_hi[:, dv * D:(dv + 1) * D], ochunk[dv],
                       bigA_lo[:, dv * D:(dv + 1) * D], ochunk[dv])
        rot = [psa(0, 1024), psa(1024, 2048), psb(0, 1024)]
        ri = 0
        for i in range(NT):
            af, fb, fbb, i_, xa, xab = t_loaded
            ah, al, ahs, als = split_tile(af, fb, fbb, i_)
            if i + 1 < NT:
                t_next = load_tile(i + 1)
            ob = (ytile, ttile)[i % 2]
            obb = gb(("ytile", "ttile")[i % 2])
            DH = min(D, 1024)
            for dh in range(cdiv(D, DH)):
                pa, pab = rot[ri % 3]
                ri += 1
                DW = min(DH, D - dh * DH)

                for dv in range(NC):
                    def mm_o(e, dh=dh, pa=pa, ahs=ahs, als=als, DH=DH, DW=DW, dv=dv):
                        last = None
                        for pi, (aop, oop) in enumerate(((ahs, bigA_hi), (ahs, bigA_lo), (als, bigA_hi))):
                            for nt in range(cdiv(DW, 512)):
                                n0, n1 = nt * 512, min(DW, (nt + 1) * 512)
                                last = e.matmul(out=pa[:, n0:n1], lhsT=aop[:, dv * 128:(dv + 1) * 128],
                                                rhs=oop[:, dv * D + dh * DH + n0:dv * D + dh * DH + n1],
                                                start=(pi == 0 and dv == 0), stop=(pi == 2 and dv == NC - 1))
                        return last
                    p.do("tensor", mm_o, reads=[ah, al, ochunk[dv]], writes=pab, inc="t")
                p.do("vector", lambda e, pa=pa, dh=dh, xa=xa, ob=ob, DH=DH, DW=DW: e.tensor_tensor(
                    out=ob[:, dh * DH:dh * DH + DW], in0=pa[:, 0:DW], in1=xa[:, dh * DH:dh * DH + DW], op=ALU.add),
                    reads=pab + [xab], writes=[obb], inc="v")
            dma_out(x2_h[i * 128:(i + 1) * 128, :], ob[:], reads=[obb], writes=[gb("x2_h")])
            if i + 1 < NT:
                t_loaded = t_next
        p.flush_block()

    prep_consts()
    if "p1" in phases and cut >= 1:
        prep_vu()
        if cut >= 2:
            ffn_phase(x_h, x1_h, gb("x_in_h"), gb("x1_h"), "x1t")
        else:
            p.flush_block()
    if "p2" in phases:
        proj_phase()
    if "p3" in phases:
        scores_phase()
    if "p4" in phases:
        av_phase()
    if "p5" in phases:
        oproj_phase()
    if "p6" in phases:
        prep_vu()
        ffn_phase(x2_h, out_h, gb("x2_h"), gb("out_hb"), "outt")
    fin = [(s, p.cnt[s], 16) for s in p.cnt if s.startswith("stB_")]
    p.do("gpsimd", lambda e: e.nop(), extra=fin)
    p.flush_block()
    nc.finalize()
    return nc


def np_softmax(x, axis=-1):
    m = x.max(axis=axis, keepdims=True)
    e = np.exp(x - m)
    return e / e.sum(axis=axis, keepdims=True)


def ref_ln(x, eps=1e-5):
    m = x.mean(-1, keepdims=True)
    v = ((x - m) ** 2).mean(-1, keepdims=True)
    return (x - m) / np.sqrt(v + eps)


def ref_ffn(y, U, V, W1, b1, W2, b2):
    h = np.maximum(y @ W1 + b1, 0)
    a = np_softmax(h @ W2 + b2)
    vx = np.einsum('krd,sd->skr', V, y)
    uvx = np.einsum('kdr,skr->skd', U, vx)
    return np.einsum('sk,skd->sd', a, uvx)


def ref_attn(xn, qkv, o_proj, gamma, beta, eps=1e-5):
    D = xn.shape[-1]
    z = ref_ln(xn, eps) * gamma + beta
    q, k, v = z @ qkv[:, :D], z @ qkv[:, D:2 * D], z @ qkv[:, 2 * D:]
    sc = (q @ k.T) / np.sqrt(np.float32(D))
    S = xn.shape[0]
    sc = np.where(np.tril(np.ones((S, S), dtype=bool)), sc, -np.inf)
    return np_softmax(sc) @ v @ o_proj + xn


def ref_full(x, qkv, o_proj, gamma, beta, U, V, W1, b1, W2, b2, upto=3):
    x = x + 0.5 * ref_ffn(ref_ln(x), U, V, W1, b1, W2, b2)
    if upto >= 2:
        x = x + ref_attn(ref_ln(x), qkv, o_proj, gamma, beta)
    if upto >= 3:
        x = x + 0.5 * ref_ffn(ref_ln(x), U, V, W1, b1, W2, b2)
    return x


def mk_inputs(S, D, R, K, seed=0):
    rng = np.random.default_rng(seed)
    f = lambda *sh: rng.standard_normal(sh).astype(np.float32)
    bq = 1.0 / np.sqrt(D)
    return dict(
        x=f(S, D), qkv=rng.uniform(-bq, bq, (D, 3 * D)).astype(np.float32),
        o_proj=rng.uniform(-bq, bq, (D, D)).astype(np.float32),
        gamma=np.ones(D, np.float32), beta=np.zeros(D, np.float32),
        U=f(K, D, R) * 0.02, V=f(K, R, D) * 0.02,
        W1=f(D, D) * 0.02, b1=np.zeros(D, np.float32),
        W2=f(D, K) * 0.02, b2=np.zeros(K, np.float32),
    )
_CACHED = {}


def _get_nc():
    if "nc" not in _CACHED:
        _CACHED["nc"] = build(dict(S=2048, D=2048, R=64, K=8))
    return _CACHED["nc"]


def kernel(L0_attn_x, L0_attn_qkv, L0_attn_o_proj, L0_attn_gamma, L0_attn_beta,
           L0_ffn_U, L0_ffn_V, L0_ffn_W1, L0_ffn_b1, L0_ffn_W2, L0_ffn_b2):
    from concourse.bass_utils import run_bass_kernel_spmd
    nc = _get_nc()
    w = dict(
        qkv=np.ascontiguousarray(L0_attn_qkv, dtype=np.float32),
        o_proj=np.ascontiguousarray(L0_attn_o_proj, dtype=np.float32),
        gamma=np.ascontiguousarray(L0_attn_gamma, dtype=np.float32),
        beta=np.ascontiguousarray(L0_attn_beta, dtype=np.float32),
        U=np.ascontiguousarray(L0_ffn_U, dtype=np.float32),
        V=np.ascontiguousarray(L0_ffn_V, dtype=np.float32),
        W1=np.ascontiguousarray(L0_ffn_W1, dtype=np.float32),
        b1=np.ascontiguousarray(L0_ffn_b1, dtype=np.float32),
        W2=np.ascontiguousarray(L0_ffn_W2, dtype=np.float32),
        b2=np.ascontiguousarray(L0_ffn_b2, dtype=np.float32),
    )
    B = L0_attn_x.shape[0]
    in_maps = [dict(w, x=np.ascontiguousarray(L0_attn_x[b], dtype=np.float32)) for b in range(B)]
    res = run_bass_kernel_spmd(nc, in_maps, core_ids=list(range(B)))
    return np.stack([res.results[b]["out"] for b in range(B)], axis=0)
